# version 41
# baseline (speedup 1.0000x reference)
import math

import numpy as np
import ml_dtypes

import concourse.bass as bass
import concourse.mybir as mybir
from concourse.bass_utils import run_bass_kernel_spmd

F32 = mybir.dt.float32
BF16 = mybir.dt.bfloat16
AF = mybir.ActivationFunctionType
ALU = mybir.AluOpType
AXX = mybir.AxisListType.X

NCORES = 8
T = 2048
D = 1024
ND = 8
NT = 16
CH = 128
H = 16
DH = 64
HA = 2048
HF = 4096
NTOK = 64 * 16384
EPS = 1e-5

_LAST_PHASE = 99
_TAPS = ()


def _gammas64():
    lin = np.linspace(np.float32(math.log(1 / 32)), np.float32(math.log(1 / 512)),
                      16, dtype=np.float32)
    return 1.0 - np.exp(lin.astype(np.float64))


def _headmap_const(vals_h):
    out = np.zeros((128, 512), np.float64)
    for h in range(H):
        out[64 * (h % 2):64 * (h % 2) + 64, (h // 2) * 64:(h // 2) * 64 + 64] = vals_h[h]
    return out


def _simcol(h):
    bank = 2 * (h % 2) + (h // 8)
    slot = (h // 2) % 4
    return bank * 512 + slot * 128


def _host_consts(rank):
    g = _gammas64()
    i = np.arange(CH)
    mask = np.zeros((128, H * CH), np.float64)
    for h in range(H):
        c0 = _simcol(h)
        mask[:, c0:c0 + CH] = np.where(
            i[None, :] >= i[:, None],
            g[h] ** np.maximum(i[None, :] - i[:, None], 0), 0.0)
    gqt = np.zeros((128, ND * CH), np.float64)
    for j in range(ND):
        for half in range(2):
            h = 2 * j + half
            gqt[64 * half:64 * half + 64, j * CH:(j + 1) * CH] = \
                (g[h] ** (i + 1.0))[None, :]
    dk = np.zeros((128, D), np.float64)
    for h in range(H):
        dk[:, h * DH:(h + 1) * DH] = (g[h] ** (127 - i.astype(np.float64)))[:, None]
    gc = _headmap_const(g ** 128.0)
    gd = np.zeros((7, 128, 512), np.float64)
    for j in range(7):
        if j < rank:
            gd[j] = _headmap_const(g ** (2048.0 * (rank - 1 - j)))
    bf = ml_dtypes.bfloat16
    return {
        "maskbf": mask.astype(bf),
        "gqt": gqt.astype(bf),
        "dkc": dk.astype(bf),
        "gc": gc.astype(np.float32),
        "gd": np.ascontiguousarray(gd.transpose(1, 0, 2)).astype(np.float32),
        "ident": np.eye(128, dtype=bf),
        "onesb_row": np.ones((1, 128), bf),
        "onesb_col": np.ones((128, 1), bf),
        "onesf_row": np.ones((1, 128), np.float32),
        "ghalf": np.repeat(np.eye(2, dtype=np.float32), 64, axis=0),
        "bhalf": np.ascontiguousarray(np.repeat(np.eye(2, dtype=np.float32), 64, axis=1)),
        "bhalf2": np.ascontiguousarray(
            np.repeat(np.eye(2, dtype=np.float32), 64, axis=1).reshape(1, 256)),
    }


def _col128(vec):
    v = np.asarray(vec, np.float32)
    k = v.size // 128
    return np.ascontiguousarray(v.reshape(k, 128).T)


_TAP_SHAPES = {
    "ln1T": ([128, ND * T], BF16),
    "s1": ([128, 8], F32), "b1": ([128, 8], F32),
    "crow": ([1, 2 * D], BF16), "cq": ([128, 8], F32),
    "kvc": ([128, 16384], BF16),
    "kscr": ([T, D], BF16), "vscr": ([T, D], BF16),
    "A": ([128, 1024], BF16), "P": ([128, 1024], BF16),
    "qT": ([128, ND * T], BF16),
    "yT": ([128, ND * T], BF16),
    "xaT": ([128, ND * T], BF16),
    "xbT": ([128, ND * T], BF16),
    "x1": ([128, NT * D], BF16),
    "ln2T": ([128, ND * T], BF16),
    "ffoT": ([128, ND * T], BF16),
    "s2": ([128, 8], F32), "b2": ([128, 8], F32),
    "a1": ([128, 32], F32), "a2": ([128, 8], F32),
}


def build_nc():
    nc = bass.Bass("TRN2", target_bir_lowering=False)
    PH = _LAST_PHASE
    taps = set(_TAPS)

    def din(name, shape, dt=F32):
        return nc.dram_tensor(name, list(shape), dt, kind="ExternalInput").ap()

    x_nat = din("x_nat", [T, D])
    qwT_d = din("qwT", [D, D]); kwT_d = din("kwT", [D, D]); vwT_d = din("vwT", [D, D])
    owT_d = din("owT", [D, D])
    w1aT_d = din("w1aT", [D, HA]); w2aT_d = din("w2aT", [HA, D])
    w1s_d = din("w1slab", [128, HF]); w2s_d = din("w2slab", [512, D])
    w1ns_d = din("w1nslab", [512, D]); w2ns_d = din("w2nslab", [128, HF])
    gn1g_d = din("gn1g", [128, 8]); gn1b_d = din("gn1b", [128, 8])
    gn2g_d = din("gn2g", [128, 8]); gn2b_d = din("gn2b", [128, 8])
    obc_d = din("obc", [128, 8]); b1ac_d = din("b1ac", [128, 16]); b2ac_d = din("b2ac", [128, 8])
    b1fc_d = din("b1fc", [128, 32]); b2fc_d = din("b2fc", [128, 8])
    maskbf_d = din("maskbf", [128, H * CH], BF16)
    gqt_d = din("gqt", [128, ND * CH], BF16)
    dkc_d = din("dkc", [128, D], BF16)
    gc_d = din("gc", [128, 512])
    gd_d = din("gd", [128, 7, 512])
    ident_d = din("ident", [128, 128], BF16)
    onesbr_d = din("onesb_row", [1, 128], BF16)
    onesbc_d = din("onesb_col", [128, 1], BF16)
    onesf_d = din("onesf_row", [1, 128])
    ghalf_d = din("ghalf", [128, 2])
    bhalf_d = din("bhalf", [2, 128])
    bhalf2_d = din("bhalf2", [1, 256])

    out_d = nc.dram_tensor("out", [T, D], F32, kind="ExternalOutput").ap()

    def dscr(name, shape, dt):
        return nc.dram_tensor(name, list(shape), dt, kind="Internal").ap()

    lnscr = dscr("lnscr", [T, D], BF16)
    kscr = dscr("kscr", [T, D], BF16)
    vscr = dscr("vscr", [T, D], BF16)
    xbscr = dscr("xbscr", [D, T], BF16)
    ln2scr = dscr("ln2scr", [T, D], BF16)
    x1scr = dscr("x1scr", [T, D], F32)
    ffoscr = dscr("ffoscr", [D, T], BF16)
    ab_d = nc.dram_tensor("abounce", [128, 512], F32, kind="Internal")
    ag_d = nc.dram_tensor("agout", [128 * NCORES, 512], F32, kind="Internal")
    gn1i_d = nc.dram_tensor("gn1i", [2, 16], F32, kind="Internal")
    gn1o_d = nc.dram_tensor("gn1o", [2, 16], F32, kind="Internal")
    g1i2_d = dscr("g1i2", [1, 32], F32)
    g1o2_d = dscr("g1o2", [1, 32], F32)
    sg1s_d = dscr("sg1s", [128, HF], BF16)
    sg2s_d = dscr("sg2s", [512, D], BF16)
    apk_d = dscr("apk", [128, 5], F32)
    sg1g_d = dscr("sg1g", [D, HF], BF16)
    sg2g_d = dscr("sg2g", [HF, D], BF16)
    apg_d = dscr("apg", [NCORES * 128, 5], F32)
    gn2i_d = nc.dram_tensor("gn2i", [2, 16], F32, kind="Internal")
    gn2o_d = nc.dram_tensor("gn2o", [2, 16], F32, kind="Internal")

    tap_d = {}
    for tname in taps:
        shp, dt = _TAP_SHAPES[tname]
        tap_d[tname] = nc.dram_tensor("tap_" + tname, list(shp), dt,
                                      kind="ExternalOutput").ap()

    def sb(name, shape, dt):
        return nc.alloc_sbuf_tensor(name, list(shape), dt)

    bigA = sb("bigA", [128, ND * T], BF16)
    bigB = sb("bigB", [128, ND * T], BF16)
    bigC = sb("bigC", [128, ND * T], BF16)
    bigD = sb("bigD", [128, ND * T], BF16)
    bigE = sb("bigE", [128, ND * T], BF16)
    scrb = sb("scrb", [128, 11264], BF16)
    apc = sb("apc", [128, 49], F32)
    maskb = sb("maskb", [128, H * CH], BF16)
    gqt_s = sb("gqt_s", [128, ND * CH], BF16)
    dk_s = sb("dk_s", [128, D], BF16)
    gc_s = sb("gc_s", [128, 512], F32)
    ones_s = sb("ones_s", [1, 128], BF16)
    onesc_s = sb("onesc_s", [128, 1], BF16)
    onesf_s = sb("onesf_s", [1, 128], F32)
    ghalf_s = sb("ghalf_s", [128, 2], F32)
    bhalf_s = sb("bhalf_s", [2, 128], F32)
    bhalf2_s = sb("bhalf2_s", [1, 256], F32)
    ident_s = sb("ident_s", [128, 128], BF16)
    bn6 = sb("bn6", [128, 60], F32)
    prm = sb("prm", [128, 104], F32)
    P_gn1g = prm[:, 0:8]; P_gn1b = prm[:, 8:16]
    P_gn2g = prm[:, 16:24]; P_gn2b = prm[:, 24:32]
    P_obc = prm[:, 32:40]; P_b1ac = prm[:, 40:56]
    P_b1fc = prm[:, 56:88]; P_b2ac = prm[:, 88:96]; P_b2fc = prm[:, 96:104]
    sml = sb("sml", [128, 104], F32)
    S_s1 = sml[:, 0:8]; S_b1 = sml[:, 8:16]
    S_s2 = sml[:, 16:24]; S_b2 = sml[:, 24:32]
    S_cq = sml[:, 32:40]; S_silu = sml[:, 40:56]
    S_a1 = sml[:, 56:88]; S_a2 = sml[:, 88:96]
    S_st = sml[:, 96:104]
    smlb = sb("smlb", [128, 16], BF16)
    B_b1bf = smlb[:, 0:8]; B_b2bf = smlb[:, 8:16]
    rows = sb("rows", [2, 576], F32)
    R_ssrow = rows[0:1, 0:512]; R_rrow = R_ssrow
    R_statloc = rows[0:2, 512:528]; R_statglob = rows[0:2, 528:544]
    R_musq = rows[0:2, 544:552]; R_var8 = rows[0:2, 552:560]
    R_bcin = rows[0:2, 560:576]
    crow = sb("crow", [1, 2 * D], BF16)
    C_k = crow[0:1, 0:D]; C_v = crow[0:1, D:2 * D]
    stt = sb("stt", [128, 1056], F32)
    A_s = stt[:, 0:512]; P_s = stt[:, 512:1024]
    part_s = stt[:, 1024:1040]
    bc_s = stt[:, 1040:1056]
    pbf = sb("pbf", [128, 512], BF16)
    rk16 = sb("rk16", [128, 16], F32)
    cpool = sb("cpool", [128, 8], F32)
    _CONSTS = [EPS, math.log(0.125), 1e-24, 1.0 / D, 1.0 / NTOK, 1.0 / HF, 1.0]

    psA = [nc.alloc_psum_tensor(f"psA{i}", [128, 512], F32) for i in range(4)]
    psB = [nc.alloc_psum_tensor(f"psB{i}", [128, 512], F32) for i in range(2)]
    psR = nc.alloc_psum_tensor("psR", [128, 512], F32)
    psS = nc.alloc_psum_tensor("psS", [128, 512], F32)

    class Sem:
        def __init__(self, name):
            self.h = nc.semaphore(name).__enter__()
            self.n = 0

        def inc(self, inst, k=1):
            inst.then_inc(self.h, k)
            self.n += k
            return self.n

        def incd(self, inst):
            inst.then_inc(self.h)
            self.n = max(self.n, 1)

    sT = Sem("sT"); sS = Sem("sS"); sV = Sem("sV"); sG = Sem("sG")
    sC1 = Sem("sC1"); sC2 = Sem("sC2"); sC3 = Sem("sC3"); sC4 = Sem("sC4")
    sLD = Sem("sLD"); sTR = Sem("sTR"); sXB = Sem("sXB"); sWQ = Sem("sWQ")
    sWG = Sem("sWG"); sOW = Sem("sOW"); sKS = Sem("sKS"); sVS = Sem("sVS")
    sTAP = Sem("sTAP"); sLDW = Sem("sLDW"); sAB = Sem("sAB")
    sX1 = Sem("sX1")
    s0 = [Sem("s0a"), Sem("s0b")]
    sXB4 = [Sem(f"sXB4_{i}") for i in range(8)]
    sXBt = [Sem(f"sXBt_{i}") for i in range(4)]
    sLW4 = [Sem(f"sLW4_{i}") for i in range(4)]
    sLW = [Sem("sLWa"), Sem("sLWb")]
    sFT = [Sem("sFTa"), Sem("sFTb")]
    sVC = [Sem("sVCa"), Sem("sVCb")]
    sKS2 = [Sem("sKS2a"), Sem("sKS2b")]
    sVS2 = [Sem("sVS2a"), Sem("sVS2b")]
    sAL = [Sem("sALa"), Sem("sALb")]
    sGN = Sem("sGN"); sWQb = Sem("sWQb"); sOT2 = Sem("sOT2")
    sAJ = [Sem("sAJa"), Sem("sAJb")]
    sOT = [Sem("sOTa"), Sem("sOTb")]
    ALL_DMA_SEMS = [sLD, sTR, sXB, sWQ, sWG, sOW, sKS, sVS, sTAP, sLDW, sAB, sX1,
                    *s0, *sLW, *sFT, *sVC, *sAL, *sAJ, *sOT, *sXB4, *sLW4,
                    *sXBt, *sKS2, *sVS2, sGN, sWQb, sOT2]

    g, sy, t, v, sc = nc.gpsimd, nc.sync, nc.tensor, nc.vector, nc.scalar

    def ve(inst):
        sV.inc(inst); v.wait_ge(sV.h, sV.n)

    def se(inst):
        sS.inc(inst); sc.wait_ge(sS.h, sS.n)

    def joinc(*engines):
        for e in engines:
            for s in (sT, sS, sV, sG):
                if s.n:
                    e.wait_ge(s.h, s.n)

    def joind(*engines):
        for e in engines:
            for s in ALL_DMA_SEMS:
                if s.n:
                    e.wait_ge(s.h, s.n)

    def r128(ap):
        return ap.rearrange("(a p) c -> p a c", p=128)

    def tap(name, src_ap):
        if name not in taps:
            return
        joinc(sy); joind(sy)
        sTAP.inc(sy.dma_start(tap_d[name], src_ap), 16)

    ln1T = bigA[:, :].rearrange("p (a c) -> p a c", c=T)
    kT = bigB[:, :].rearrange("p (a c) -> p a c", c=T)
    qT = bigC[:, :].rearrange("p (a c) -> p a c", c=T)
    yT = bigD[:, :].rearrange("p (a c) -> p a c", c=T)
    kvc = bigE[:, :].bitcast(F32).rearrange("p (a c) -> p a c", c=512)
    x_aT = qT
    x_bT = yT
    ln2T = yT
    ffoT = bigE[:, :].rearrange("p (a c) -> p a c", c=T)
    hA = bigA[:, 0:8192].rearrange("p (a c) -> p a c", c=512)
    hq = bigA[:, 8192:12288].rearrange("p (a c) -> p a c", c=512)
    w1aTs = bigB[:, :].rearrange("p (a c) -> p a c", c=HA)
    w2aTs = bigE[:, :].rearrange("p (a c) -> p a c", c=D)
    w1fq = bigC[:, 0:8192].rearrange("p (a c) -> p a c", c=D)
    w2fq = bigC[:, 8192:16384].rearrange("p (a c) -> p a c", c=D)
    kwTs = bigB[:, 0:8192].rearrange("p (a c) -> p a c", c=D)
    vwTs = bigB[:, 8192:16384].rearrange("p (a c) -> p a c", c=D)
    qwTs = bigD[:, 0:8192].rearrange("p (a c) -> p a c", c=D)

    for dst, src in ((maskb.ap(), maskbf_d), (gqt_s.ap(), gqt_d), (dk_s.ap(), dkc_d),
                     (gc_s.ap(), gc_d),
                     (ones_s.ap(), onesbr_d), (onesc_s.ap(), onesbc_d),
                     (onesf_s.ap(), onesf_d), (ghalf_s.ap(), ghalf_d),
                     (bhalf_s.ap(), bhalf_d), (bhalf2_s.ap(), bhalf2_d),
                     (ident_s.ap(), ident_d)):
        sLD.inc(sy.dma_start(dst, src), 16)
    for dst, src in ((prm[:, 0:8], gn1g_d), (prm[:, 8:16], gn1b_d),
                     (prm[:, 16:24], gn2g_d), (prm[:, 24:32], gn2b_d),
                     (prm[:, 32:40], obc_d), (prm[:, 40:56], b1ac_d),
                     (prm[:, 56:88], b1fc_d), (prm[:, 88:96], b2ac_d),
                     (prm[:, 96:104], b2fc_d)):
        sLD.inc(sy.dma_start(dst, src), 16)

    sG.inc(g.memset(A_s, 0.0))
    sG.inc(g.memset(P_s, 0.0))
    for ci, cv in enumerate(_CONSTS):
        sG.inc(g.memset(cpool[:, ci:ci + 1], cv))
        nc.const_aps.register(F32, cv, cpool[:, ci:ci + 1])
    consts_g = sG.n

    x0rot4 = scrb[:, 0:8192].rearrange("p (a c) -> p a c", c=D)
    junk1 = scrb[:, 8192:10240]

    def lnstats(src_ap, junk):
        se(sc.activation(junk, src_ap, AF.Identity, accum_out=S_st[:, 2:3]))
        sS.inc(sc.activation(junk, src_ap, AF.Square, accum_out=S_st[:, 3:4]))
        v.wait_ge(sS.h, sS.n)
        ve(v.tensor_scalar_mul(S_st[:, 0:1], S_st[:, 2:3], 1.0 / D))
        ve(v.tensor_scalar_mul(S_st[:, 4:5], S_st[:, 0:1], S_st[:, 0:1]))
        sV.inc(v.tensor_scalar(S_st[:, 1:2], S_st[:, 3:4], 1.0 / D, S_st[:, 4:5],
                               ALU.mult, ALU.subtract))
        sc.wait_ge(sV.h, sV.n)
        se(sc.activation(S_st[:, 6:7], S_st[:, 1:2], AF.Ln, bias=EPS))
        sS.inc(sc.activation(S_st[:, 5:6], S_st[:, 6:7], AF.Exp, scale=-0.5))
        v.wait_ge(sS.h, sS.n)

    def lnstats_v(src_ap):
        v.bn_stats(bn6[:, 0:6], src_ap[:, 0:512])
        ve(v.bn_stats(bn6[:, 6:12], src_ap[:, 512:1024]))
        sV.inc(v.bn_aggr(S_st[:, 0:2], bn6[:, 0:12]))
        sc.wait_ge(sV.h, sV.n)
        se(sc.activation(S_st[:, 6:7], S_st[:, 1:2], AF.Ln, bias=EPS))
        sS.inc(sc.activation(S_st[:, 5:6], S_st[:, 6:7], AF.Exp, scale=-0.5))
        v.wait_ge(sS.h, sS.n)

    sqf2 = [bigC[:, 0:2048].bitcast(F32), bigC[:, 2048:4096].bitcast(F32)]
    onesfc = cpool[:, 6:7]
    R2 = rows[0:1, 512:576]
    if PH >= 1:
        v.wait_ge(sG.h, consts_g)
        sc.wait_ge(sG.h, consts_g)
        t.wait_ge(sG.h, consts_g)
        t.wait_ge(sLD.h, sLD.n)
        tile_mm = [0] * 8
        tile_sq = [0] * 8
        sq_rd = [0, 0]
        tr_ev = [0, 0]
        prev_var = 0
        for gi in range(4):
            for i4 in range(4):
                ti = 4 * gi + i4
                i8 = ti % 8
                if tile_mm[i8]:
                    g.wait_ge(sT.h, tile_mm[i8])
                    g.wait_ge(sS.h, tile_sq[i8])
                sXB4[i8].inc(g.dma_start(x0rot4[:, i8, :],
                                         x_nat[ti * 128:(ti + 1) * 128, :]), 16)
            for i4 in range(4):
                ti = 4 * gi + i4
                i8 = ti % 8
                v.wait_ge(sXB4[i8].h, sXB4[i8].n)
                if i4 == 0 and prev_var:
                    v.wait_ge(sV.h, prev_var)
                v.bn_stats(bn6[:, 12 * i4:12 * i4 + 6], x0rot4[:, i8, 0:512])
                ve(v.bn_stats(bn6[:, 12 * i4 + 6:12 * i4 + 12],
                              x0rot4[:, i8, 512:1024]))
                ve(v.bn_aggr(bn6[:, 48 + 2 * i4:50 + 2 * i4],
                             bn6[:, 12 * i4:12 * i4 + 12]))
                sc.wait_ge(sV.h, sV.n)
                se(sc.activation(bn6[:, 56 + i4:57 + i4],
                                 bn6[:, 49 + 2 * i4:50 + 2 * i4], AF.Ln,
                                 bias=EPS))
                sS.inc(sc.activation(bn6[:, 56 + i4:57 + i4],
                                     bn6[:, 56 + i4:57 + i4], AF.Exp,
                                     scale=-0.5))
                v.wait_ge(sS.h, sS.n)
                sV.inc(v.tensor_scalar(x0rot4[:, i8, :], x0rot4[:, i8, :],
                                       bn6[:, 48 + 2 * i4:49 + 2 * i4],
                                       bn6[:, 56 + i4:57 + i4],
                                       ALU.subtract, ALU.mult))
                apl = sV.n
                sqs = sqf2[ti % 2]
                if sq_rd[ti % 2]:
                    sc.wait_ge(sT.h, sq_rd[ti % 2])
                sc.wait_ge(sV.h, apl)
                sS.inc(sc.activation(sqs, x0rot4[:, i8, :], AF.Square))
                tile_sq[i8] = sS.n
                t.wait_ge(sV.h, apl)
                for hb in range(2):
                    t.matmul(psA[hb][0:1, 0:512], onesc_s.ap(),
                             x0rot4[:, i8, hb * 512:(hb + 1) * 512],
                             start=(ti == 0), stop=(ti == 15))
                for half in range(2):
                    bank = psB[half]
                    if tr_ev[half]:
                        t.wait_ge(sS.h, tr_ev[half])
                    for j4 in range(4):
                        jb = half * 4 + j4
                        last = t.matmul(bank[:, j4 * 128:(j4 + 1) * 128],
                                        x0rot4[:, i8, jb * 128:(jb + 1) * 128],
                                        ident_s.ap(), start=True, stop=True)
                    sT.inc(last)
                    sc.wait_ge(sT.h, sT.n)
                    sS.inc(sc.copy(
                        ln1T[:, half * 4:half * 4 + 4, ti * 128:(ti + 1) * 128],
                        bank[:, 0:512]))
                    tr_ev[half] = sS.n
                tile_mm[i8] = sT.n
                t.wait_ge(sS.h, tile_sq[i8])
                last = None
                for hb in range(2):
                    last = t.matmul(psA[2 + hb][0:1, 0:512], onesfc,
                                    sqs[:, hb * 512:(hb + 1) * 512],
                                    start=(ti == 0), stop=(ti == 15))
                sT.inc(last)
                sq_rd[ti % 2] = sT.n
                if i4 == 3:
                    prev_var = sV.n
        v.wait_ge(sT.h, sT.n)
        last = None
        for st_, cb in ((0, 0), (2, 16)):
            for gg in range(16):
                bk = st_ + gg // 8
                col = cb + (gg % 2) * 8 + gg // 2
                last = v.tensor_reduce(R2[:, col:col + 1],
                                       psA[bk][0:1, (gg % 8) * 64:(gg % 8) * 64 + 64],
                                       AXX, ALU.add)
        sV.inc(last)
        grp_done = sV.n
        g.wait_ge(sV.h, grp_done)
        sWG.inc(g.dma_start(g1i2_d, R2[:, 0:32]), 16)
        g.wait_ge(sWG.h, sWG.n)
        sC1.incd(g.collective_compute(
            "AllReduce", ALU.add, replica_groups=[list(range(NCORES))],
            ins=[g1i2_d.opt()], outs=[g1o2_d.opt()]))
        sLDW.inc(g.dma_start(kwTs, r128(kwT_d)), 16)
        sLDW.inc(g.dma_start(vwTs, r128(vwT_d)), 16)
        sLDW.inc(g.dma_start(qwTs, r128(qwT_d)), 16)
        g.wait_ge(sC1.h, 1)
        sWG.inc(g.dma_start(R2[:, 0:32], g1o2_d), 16)
        gn1_ld = sWG.n
        ln1T_done = sS.n
        tap("ln1T", bigA[:, :])

    sqb2 = scrb[:, 8192:10240]

    def gn_partials(srcT, j):
        ve(v.tensor_reduce(part_s[:, j:j + 1], srcT[:, j, :], AXX, ALU.add))
        se(sc.activation(sqb2, srcT[:, j, :], AF.Square,
                         accum_out=part_s[:, 8 + j:9 + j]))

    def gn_stats_fold(srcT, gni_dram, gno_dram, ccsem, Pg, Pb, Ss, Sb, Bbf,
                      skip_partials=False):
        if not skip_partials:
            for j in range(ND):
                gn_partials(srcT, j)
        sV.inc(v.tensor_copy(bn6[:, 0:1], part_s[:, 0:1]))
        t.wait_ge(sV.h, sV.n)
        t.wait_ge(sS.h, sS.n)
        sT.inc(t.matmul(psR[0:2, 0:16], ghalf_s.ap(), part_s, start=True, stop=True))
        sc.wait_ge(sT.h, sT.n)
        sS.inc(sc.copy(R_statloc, psR[0:2, 0:16]))
        g.wait_ge(sS.h, sS.n)
        sWG.inc(g.dma_start(gni_dram.ap(), R_statloc), 16)
        g.wait_ge(sWG.h, sWG.n)
        ccsem.incd(g.collective_compute(
            "AllReduce", ALU.add, replica_groups=[list(range(NCORES))],
            ins=[gni_dram.ap().opt()], outs=[gno_dram.ap().opt()]))
        g.wait_ge(ccsem.h, 1)
        sWG.inc(g.dma_start(R_statglob, gno_dram.ap()), 16)
        v.wait_ge(sWG.h, sWG.n)
        ve(v.tensor_scalar_mul(R_statglob, R_statglob, 1.0 / NTOK))
        ve(v.tensor_tensor(R_musq, R_statglob[:, 0:8], R_statglob[:, 0:8], ALU.mult))
        ve(v.tensor_sub(R_var8, R_statglob[:, 8:16], R_musq))
        sV.inc(v.tensor_copy(R_bcin[:, 0:8], R_statglob[:, 0:8]))
        sc.wait_ge(sV.h, sV.n)
        se(sc.activation(R_bcin[:, 8:16], R_var8, AF.Ln, bias=EPS))
        sS.inc(sc.activation(R_bcin[:, 8:16], R_bcin[:, 8:16], AF.Exp, scale=-0.5))
        t.wait_ge(sS.h, sS.n)
        sT.inc(t.matmul(psR[0:128, 0:16], bhalf_s.ap(), R_bcin, start=True, stop=True))
        sc.wait_ge(sT.h, sT.n)
        sS.inc(sc.copy(bc_s, psR[0:128, 0:16]))
        v.wait_ge(sS.h, sS.n)
        ve(v.tensor_tensor(Ss, bc_s[:, 8:16], Pg, ALU.mult))
        ve(v.tensor_tensor(part_s[:, 0:8], bc_s[:, 0:8], Ss, ALU.mult))
        sV.inc(v.tensor_sub(Sb, Pb, part_s[:, 0:8]))
        sc.wait_ge(sV.h, sV.n)
        sS.inc(sc.copy(Bbf, Sb))

    stg1 = bigC[:, 0:4096]
    stg2f = bigC[:, 4096:8192]
    stg2 = stg2f.rearrange("p (a c) -> p a c", c=D)
    scrf = bigC[:, 8192:16384].bitcast(F32)
    if PH >= 2:
        g.wait_ge(sLDW.h, sLDW.n)
        g.wait_ge(sT.h, max(sq_rd))
        g.wait_ge(sS.h, max(tile_sq))
        sAL[0].inc(g.dma_start(stg1, w1s_d), 16)
        sAL[1].inc(g.dma_start(stg2, r128(w2s_d)), 16)
        stg_ld = (sAL[0].n, sAL[1].n)
        g.wait_ge(sAL[0].h, sAL[0].n)
        sAL[0].inc(g.dma_start(scrf.rearrange("p (a c) -> p a c", c=D),
                               r128(w1ns_d)), 16)
        w1n_ld = sAL[0].n
    if PH >= 2:
        v.wait_ge(sWG.h, gn1_ld)
        v.wait_ge(sLD.h, sLD.n)
        ve(v.tensor_scalar_mul(R2[:, 32:48], R2[:, 0:16], 1.0 / NTOK))
        ve(v.tensor_tensor(R2[:, 0:16], R2[:, 32:48], R2[:, 32:48], ALU.mult))
        ve(v.tensor_scalar_mul(R2[:, 16:32], R2[:, 16:32], 1.0 / NTOK))
        sV.inc(v.tensor_sub(R2[:, 16:32], R2[:, 16:32], R2[:, 0:16]))
        sc.wait_ge(sV.h, sV.n)
        se(sc.activation(R2[:, 48:64], R2[:, 16:32], AF.Ln, bias=EPS))
        sS.inc(sc.activation(R2[:, 48:64], R2[:, 48:64], AF.Exp, scale=-0.5))
        t.wait_ge(sS.h, sS.n)
        t.wait_ge(sLD.h, sLD.n)
        t.matmul(psR[0:128, 0:8], bhalf2_s.ap()[:, 0:128], R2[:, 32:40],
                 start=True, stop=False)
        t.matmul(psR[0:128, 0:8], bhalf2_s.ap()[:, 128:256], R2[:, 40:48],
                 start=False, stop=True)
        t.matmul(psR[0:128, 8:16], bhalf2_s.ap()[:, 0:128], R2[:, 48:56],
                 start=True, stop=False)
        sT.inc(t.matmul(psR[0:128, 8:16], bhalf2_s.ap()[:, 128:256],
                        R2[:, 56:64], start=False, stop=True))
        sc.wait_ge(sT.h, sT.n)
        sS.inc(sc.copy(bc_s, psR[0:128, 0:16]))
        v.wait_ge(sS.h, sS.n)
        ve(v.tensor_tensor(S_s1, bc_s[:, 8:16], P_gn1g, ALU.mult))
        ve(v.tensor_tensor(part_s[:, 0:8], bc_s[:, 0:8], S_s1, ALU.mult))
        sV.inc(v.tensor_sub(S_b1, P_gn1b, part_s[:, 0:8]))
        sc.wait_ge(sV.h, sV.n)
        sS.inc(sc.copy(B_b1bf, S_b1))
        v.wait_ge(sLDW.h, sLDW.n)
        t.wait_ge(sS.h, sS.n)
        t.wait_ge(sLD.h, sLD.n)
        t.wait_ge(sLDW.h, sLDW.n)
        for wsrc, cdst in ((kwTs, C_k), (vwTs, C_v)):
            t.wait_ge(sS.h, sS.n)
            last = None
            for nb2 in range(2):
                reg = psS[32 * nb2:32 * nb2 + 1, 0:512]
                for kk in range(ND):
                    last = t.matmul(reg, B_b1bf[:, kk:kk + 1],
                                    wsrc[:, kk, nb2 * 512:(nb2 + 1) * 512],
                                    start=(kk == 0), stop=(kk == ND - 1),
                                    tile_position=(0, 32 * nb2))
            sT.inc(last)
            sc.wait_ge(sT.h, sT.n)
            sS.inc(sc.copy(cdst[0:1, 0:512], psS[0:1, 0:512]))
            sS.inc(sc.copy(cdst[0:1, 512:1024], psS[32:33, 0:512]))
        last = None
        for m in range(ND):
            for kk in range(ND):
                last = t.matmul(psR[:, 16 + m:17 + m],
                                qwTs[:, kk, m * 128:(m + 1) * 128],
                                B_b1bf[:, kk:kk + 1],
                                start=(kk == 0), stop=(kk == ND - 1))
        sT.inc(last)
        sc.wait_ge(sT.h, sT.n)
        sS.inc(sc.copy(S_cq, psR[:, 16:24]))
        v.wait_ge(sT.h, sT.n)
        for w in (qwTs, kwTs, vwTs):
            for kk in range(ND):
                v.tensor_scalar_mul(w[:, kk, :], w[:, kk, :], S_s1[:, kk:kk + 1])
        sV.inc(v.tensor_copy(bn6[:, 0:1], S_s1[:, 0:1]))
        wscale_v = sV.n
        tap("s1", S_s1); tap("b1", S_b1); tap("crow", crow.ap()); tap("cq", S_cq)

        sc.wait_ge(sAL[0].h, stg_ld[0])
        sc.wait_ge(sAL[1].h, stg_ld[1])
        se(sc.sign(stg1, stg1))
        sS.inc(sc.sign(stg2f, stg2f))
        sc.wait_ge(sS.h, sS.n)
        sGN.inc(sc.dma_start(sg1s_d, stg1), 16)
        sGN.inc(sc.dma_start(r128(sg2s_d), stg2), 16)
        sc.wait_ge(sAL[0].h, w1n_ld)
        for a_ in range(4):
            se(sc.activation(scrf[:, a_ * D:(a_ + 1) * D],
                             scrf[:, a_ * D:(a_ + 1) * D], AF.Abs,
                             accum_out=apc[:, a_:a_ + 1]))
        g.wait_ge(sS.h, sS.n)
        g.wait_ge(sAL[1].h, sAL[1].n)
        sAL[1].inc(g.dma_start(scrf, w2ns_d), 16)
        sc.wait_ge(sAL[1].h, sAL[1].n)
        for a_ in range(4):
            se(sc.activation(scrf[:, a_ * D:(a_ + 1) * D],
                             scrf[:, a_ * D:(a_ + 1) * D], AF.Abs,
                             accum_out=apc[:, 5 + a_:6 + a_]))
        v.wait_ge(sS.h, sS.n)
        sV.inc(v.tensor_reduce(apc[:, 4:5], apc[:, 5:9], AXX, ALU.add))
        sc.wait_ge(sV.h, sV.n)
        sGN.inc(sc.dma_start(apk_d, apc[:, 0:5]), 16)
        sgn_stores = sGN.n
        g.wait_ge(sGN.h, sgn_stores)
        for cin, cout in ((apk_d, apg_d), (sg1s_d, sg1g_d), (sg2s_d, sg2g_d)):
            sC4.incd(g.collective_compute(
                "AllGather", ALU.bypass, replica_groups=[list(range(NCORES))],
                ins=[cin.opt()], outs=[cout.opt()]))

    kraw2 = [scrb[:, 0:1024], scrb[:, 1024:2048]]
    k2b2 = [scrb[:, 2048:3072], scrb[:, 3072:4096]]
    vbf2 = [scrb[:, 4096:5120], scrb[:, 5120:6144]]
    jnk3a = scrb[:, 6144:7168]

    if PH >= 3:
        t.wait_ge(sV.h, wscale_v)
        t.wait_ge(sV.h, grp_done)
        t.wait_ge(sS.h, ln1T_done)
        t.wait_ge(sS.h, sS.n)
        evk = [0, 0, 0, 0]
        evv = [0, 0]
        kv_evac = 0
        kslot_free = [0, 0]
        kslot_dma = [0, 0]
        k2_read = [0, 0]
        vslot_read = [0, 0]
        vslot_dma = [0, 0]
        kv_read_v = 0
        for c in range(NT):
            sl = c % 2
            kraw = kraw2[sl]
            k2b = k2b2[sl]
            vbf = vbf2[sl]
            bk = 2 * sl
            last = None
            for nb2 in range(2):
                if evk[bk + nb2]:
                    t.wait_ge(sS.h, evk[bk + nb2])
                reg = psA[bk + nb2]
                for kk in range(ND):
                    t.matmul(reg[:, :], ln1T[:, kk, c * 128:(c + 1) * 128],
                             kwTs[:, kk, nb2 * 512:(nb2 + 1) * 512],
                             start=(kk == 0), stop=False)
                last = t.matmul(reg[:, :], ones_s.ap(),
                                C_k[0:1, nb2 * 512:(nb2 + 1) * 512],
                                start=False, stop=True)
            sT.inc(last)
            kmm = sT.n
            for nb2 in range(2):
                reg = psB[1] if nb2 == 0 else psR
                if evv[nb2]:
                    t.wait_ge(sS.h, evv[nb2])
                for kk in range(ND):
                    t.matmul(reg[:, :], ln1T[:, kk, c * 128:(c + 1) * 128],
                             vwTs[:, kk, nb2 * 512:(nb2 + 1) * 512],
                             start=(kk == 0), stop=False)
                last = t.matmul(reg[:, :], ones_s.ap(),
                                C_v[0:1, nb2 * 512:(nb2 + 1) * 512],
                                start=False, stop=True)
            sT.inc(last)
            vmm = sT.n
            sc.wait_ge(sT.h, kmm)
            if kslot_free[sl]:
                sc.wait_ge(sV.h, kslot_free[sl])
                sc.wait_ge(sKS2[sl].h, kslot_dma[sl])
            sc.copy(kraw[:, 0:512], psA[bk][:, :])
            se(sc.copy(kraw[:, 512:1024], psA[bk + 1][:, :]))
            evk[bk] = evk[bk + 1] = sS.n
            se(sc.activation(jnk3a, kraw, AF.Square, accum_out=S_st[:, 6:7]))
            se(sc.activation(S_st[:, 7:8], S_st[:, 6:7], AF.Ln, bias=1e-24))
            sS.inc(sc.activation(rk16[:, c:c + 1], S_st[:, 7:8], AF.Exp,
                                 scale=-0.5))
            k_done = sS.n
            sy.wait_ge(sS.h, k_done)
            sKS2[sl].inc(sy.dma_start(kscr[c * 128:(c + 1) * 128, :], kraw), 16)
            kslot_dma[sl] = sKS2[sl].n
            sc.wait_ge(sT.h, vmm)
            if vslot_read[sl]:
                sc.wait_ge(sT.h, vslot_read[sl])
                sc.wait_ge(sVS2[sl].h, vslot_dma[sl])
            sc.copy(vbf[:, 0:512], psB[1][:, :])
            sS.inc(sc.copy(vbf[:, 512:1024], psR[:, :]))
            evv[0] = evv[1] = sS.n
            vbf_done = sS.n
            sy.wait_ge(sS.h, vbf_done)
            sVS2[sl].inc(sy.dma_start(vscr[c * 128:(c + 1) * 128, :], vbf), 16)
            vslot_dma[sl] = sVS2[sl].n
            v.wait_ge(sS.h, k_done)
            if k2_read[sl]:
                v.wait_ge(sT.h, k2_read[sl])
            sV.inc(v.scalar_tensor_tensor(k2b, kraw, rk16[:, c:c + 1],
                                          dk_s.ap(), ALU.mult, ALU.mult))
            k2_done = sV.n
            kslot_free[sl] = sV.n
            t.wait_ge(sV.h, k2_done)
            t.wait_ge(sS.h, vbf_done)
            if kv_evac:
                t.wait_ge(sS.h, kv_evac)
            last = None
            for h in range(H):
                pb = 64 * (h % 2)
                last = t.matmul(psB[0][pb:pb + 64, (h // 2) * 64:(h // 2) * 64 + 64],
                                k2b[:, h * 64:(h + 1) * 64],
                                vbf[:, h * 64:(h + 1) * 64],
                                start=True, stop=True, tile_position=(0, pb))
            sT.inc(last)
            k2_read[sl] = sT.n
            vslot_read[sl] = sT.n
            sc.wait_ge(sT.h, sT.n)
            if kv_read_v:
                sc.wait_ge(sV.h, kv_read_v)
            sS.inc(sc.copy(kvc[:, c, :], psB[0][:, :]))
            kv_evac = sS.n
            v.wait_ge(sS.h, kv_evac)
            ve(v.tensor_tensor(A_s, A_s, gc_s.ap(), ALU.mult))
            ve(v.tensor_tensor(A_s, A_s, kvc[:, c, :], ALU.add))
            kv_read_v = sV.n
        p3a_mm_end = sT.n
        a_done = sV.n
        sy.wait_ge(sC4.h, 1)
        apl = apc[:, 9:49].rearrange("p (r a) -> p r a", a=5)
        sGN.inc(sy.dma_start(apl, r128(apg_d)), 16)
        v.wait_ge(sGN.h, sGN.n)
        ve(v.tensor_scalar_mul(
            S_a1.rearrange("p (r a) -> p r a", a=4), apl[:, :, 0:4], 1.0 / D))
        ve(v.tensor_scalar_mul(S_a2, apl[:, :, 4], 1.0 / HF))
        alpha_done = sV.n
        tap("a1", S_a1); tap("a2", S_a2)
        tap("kscr", kscr); tap("vscr", vscr)
        tap("kvc", bigE[:, 0:16384])
        tap("A", stt[:, 0:512].bitcast(BF16))

        sy.wait_ge(sV.h, a_done)
        sAB.inc(sy.dma_start(ab_d.ap(), A_s), 16)
        g.wait_ge(sAB.h, sAB.n)
        sC2.incd(g.collective_compute(
            "AllGather", ALU.bypass, replica_groups=[list(range(NCORES))],
            ins=[ab_d.ap().opt()], outs=[ag_d.ap().opt()]))

    qtmp = scrb[:, 0:4096].rearrange("p (a c) -> p a c", c=512)
    sqb8 = scrb[:, 4096:8192].rearrange("p (a c) -> p a c", c=512)
    repf = scrb[:, 8192:9216].bitcast(F32)

    if PH >= 4:
        joinc(t, sc, v)
        joind(t, sc, v)
        evq = [0, 0, 0, 0]
        ssrow_v = 0
        rep_evac = 0
        qtmp_scaled = 0
        prev_ssmm = 0
        gd_s = bigD[:, 8192:15360].bitcast(F32)
        ajrot = scrb[:, 9216:11264].bitcast(F32)
        p_init_done = 0
        for nb in range(4):
            sq_v = [0] * ND
            for m in range(ND):
                reg = psA[m % 4]
                if evq[m % 4]:
                    t.wait_ge(sS.h, evq[m % 4])
                last = None
                for kk in range(ND):
                    last = t.matmul(reg[:, :], qwTs[:, kk, m * 128:(m + 1) * 128],
                                    ln1T[:, kk, nb * 512:(nb + 1) * 512],
                                    start=(kk == 0), stop=(kk == ND - 1))
                sT.inc(last)
                sc.wait_ge(sT.h, sT.n)
                if qtmp_scaled:
                    sc.wait_ge(sV.h, qtmp_scaled)
                sS.inc(sc.activation(qtmp[:, m, :], reg[:, :], AF.Identity,
                                     bias=S_cq[:, m:m + 1]))
                evq[m % 4] = sS.n
                v.wait_ge(sS.h, sS.n)
                if m == 0 and prev_ssmm:
                    v.wait_ge(sT.h, prev_ssmm)
                sV.inc(v.tensor_tensor(sqb8[:, m, :], qtmp[:, m, :],
                                       qtmp[:, m, :], ALU.mult))
                sq_v[m] = sV.n
            if ssrow_v:
                t.wait_ge(sV.h, ssrow_v)
            for m in range(ND):
                t.wait_ge(sV.h, sq_v[m])
                sT.inc(t.matmul(psR[0:1, 0:512], onesc_s.ap(), sqb8[:, m, :],
                                start=(m == 0), stop=(m == ND - 1)))
            prev_ssmm = sT.n
            v.wait_ge(sT.h, sT.n)
            ve(v.tensor_scalar_max(R_ssrow, psR[0:1, 0:512], 1e-24))
            ssrow_v = sV.n
            sc.wait_ge(sV.h, sV.n)
            se(sc.activation(R_rrow, R_ssrow, AF.Ln))
            sS.inc(sc.activation(R_rrow, R_rrow, AF.Exp, scale=-0.5,
                                 bias=math.log(0.125)))
            t.wait_ge(sS.h, sS.n)
            if rep_evac:
                t.wait_ge(sS.h, rep_evac)
            sT.inc(t.matmul(psB[0][:, :], onesf_s.ap(), R_rrow, start=True, stop=True))
            sc.wait_ge(sT.h, sT.n)
            sS.inc(sc.copy(repf, psB[0][:, :]))
            rep_evac = sS.n
            v.wait_ge(sS.h, rep_evac)
            last = None
            for m in range(ND):
                last = v.tensor_tensor(qT[:, m, nb * 512:(nb + 1) * 512],
                                       qtmp[:, m, :], repf, ALU.mult)
            sV.inc(last)
            qtmp_scaled = sV.n
            if nb == 1:
                g.wait_ge(sC2.h, 1)
                sWG.inc(g.dma_start(gd_s[:, 0:3584],
                                    gd_d.rearrange("p a c -> p (a c)")), 16)
                gd_ld = sWG.n
                aj_use = [0, 0]
                for j in range(NCORES - 1):
                    sl = j % 2
                    ajs = ajrot[:, sl * 512:(sl + 1) * 512]
                    if aj_use[sl]:
                        g.wait_ge(sV.h, aj_use[sl])
                    sAJ[sl].inc(g.dma_start(
                        ajs, ag_d.ap()[j * 128:(j + 1) * 128, :]), 16)
                    v.wait_ge(sAJ[sl].h, sAJ[sl].n)
                    if j == 0:
                        v.wait_ge(sWG.h, gd_ld)
                    ve(v.tensor_tensor(ajs, ajs,
                                       gd_s[:, j * 512:(j + 1) * 512], ALU.mult))
                    ve(v.tensor_tensor(P_s, P_s, ajs, ALU.add))
                    aj_use[sl] = sV.n
                p_init_done = sV.n
        q_done = sV.n
        tap("qT", bigC[:, :])
        tap("P", stt[:, 512:1024].bitcast(BF16))

    owTs = bigB[:, 0:16384].rearrange("p (a c) -> p a c", c=T)[:, :, 0:1024]
    simbf = scrb[:, 0:2048]
    q2b = scrb[:, 2048:3072]
    vchunk = scrb[:, 3072:5120]

    if PH >= 5:
        sy.wait_ge(sKS2[0].h, sKS2[0].n)
        sy.wait_ge(sKS2[1].h, sKS2[1].n)
        sy.wait_ge(sT.h, p3a_mm_end)
        for j in range(ND):
            sTR.inc(sy.dma_start(kT[:, j, :], kscr[:, j * 128:(j + 1) * 128],
                                 transpose=True), 16)
        kT_ld = sTR.n
        joinc(t)
        q2v = q2b.rearrange("p (a c) -> p a c", c=128)
        gq3 = gqt_s.ap().rearrange("p (a c) -> p a c", c=128)

        sy.wait_ge(sVS2[0].h, sVS2[0].n)
        sy.wait_ge(sVS2[1].h, sVS2[1].n)
        sy.wait_ge(sV.h, q_done)
        sy.wait_ge(sT.h, prev_ssmm)
        sVC[0].inc(sy.dma_start(vchunk[:, 0:1024], vscr[0:128, :]), 16)
        vld = [sVC[0].n, 0]
        mask_done = [0, 0, 0, 0]
        yevac = [0, 0]
        p_up = p_init_done
        pbf_read = 0
        q2_done = 0
        vread = [0, 0]
        for c in range(NT):
            sl = c % 2
            if c + 1 < NT:
                if vread[1 - sl]:
                    sy.wait_ge(sT.h, vread[1 - sl])
                sVC[1 - sl].inc(sy.dma_start(
                    vchunk[:, (1 - sl) * 1024:(2 - sl) * 1024],
                    vscr[(c + 1) * 128:(c + 2) * 128, :]), 16)
                vld[1 - sl] = sVC[1 - sl].n
            if q2_done:
                v.wait_ge(sT.h, q2_done)
            if c == 0:
                v.wait_ge(sV.h, q_done)
                v.wait_ge(sT.h, p3a_mm_end)
            sV.inc(v.tensor_tensor(q2v, qT[:, :, c * 128:(c + 1) * 128], gq3,
                                   ALU.mult))
            q2r = sV.n
            sc.wait_ge(sV.h, p_up)
            if pbf_read:
                sc.wait_ge(sT.h, pbf_read)
            sS.inc(sc.activation(pbf.ap(), P_s, AF.Identity))
            pbf_s = sS.n
            t.wait_ge(sTR.h, kT_ld)
            for b in (0, 2, 1, 3):
                heads = [hh * 2 + (b // 2) for hh in range(4 * (b % 2), 4 * (b % 2) + 4)]
                pb = 64 * (b // 2)
                if mask_done[b]:
                    t.wait_ge(sV.h, mask_done[b])
                last = None
                for slot, h in enumerate(heads):
                    last = t.matmul(psA[b][:, slot * 128:(slot + 1) * 128],
                                    kT[pb:pb + 64, h // 2, c * 128:(c + 1) * 128],
                                    qT[pb:pb + 64, h // 2, c * 128:(c + 1) * 128],
                                    start=True, stop=True, tile_position=(pb, 0))
                sT.inc(last)
                v.wait_ge(sT.h, sT.n)
                sV.inc(v.scalar_tensor_tensor(
                    simbf[:, b * 512:(b + 1) * 512], psA[b][:, :],
                    rk16[:, c:c + 1], maskb[:, b * 512:(b + 1) * 512],
                    ALU.mult, ALU.mult))
                mask_done[b] = sV.n
            t.wait_ge(sV.h, q2r)
            t.wait_ge(sS.h, pbf_s)
            t.wait_ge(sVC[sl].h, vld[sl])
            for bank in range(2):
                if yevac[bank]:
                    t.wait_ge(sS.h, yevac[bank])
                t.wait_ge(sV.h, mask_done[2 + bank])
                last = None
                for hh in range(8):
                    h = 8 * bank + hh
                    pb = 64 * (h % 2)
                    j = h // 2
                    reg = psB[bank][pb:pb + 64, (j % 4) * 128:(j % 4) * 128 + 128]
                    sc0 = _simcol(h)
                    t.matmul(reg,
                             vchunk[:, sl * 1024 + h * 64:sl * 1024 + (h + 1) * 64],
                             simbf[:, sc0:sc0 + 128],
                             start=True, stop=False, tile_position=(0, pb))
                    last = t.matmul(reg, pbf[pb:pb + 64, j * 64:(j + 1) * 64],
                                    q2v[pb:pb + 64, j, :],
                                    start=False, stop=True, tile_position=(pb, pb))
                sT.inc(last)
                sc.wait_ge(sT.h, sT.n)
                ps3 = psB[bank][:, :].rearrange("p (a c) -> p a c", c=128)
                y3 = yT[:, 4 * bank:4 * bank + 4, c * 128:(c + 1) * 128]
                sS.inc(sc.activation(y3, ps3, AF.Identity))
                yevac[bank] = sS.n
            pbf_read = sT.n
            vread[sl] = sT.n
            q2_done = sT.n
            v.wait_ge(sS.h, pbf_s)
            ve(v.tensor_tensor(P_s, P_s, gc_s.ap(), ALU.mult))
            ve(v.tensor_tensor(P_s, P_s, kvc[:, c, :], ALU.add))
            p_up = sV.n
            if c == 7:
                kt8_free = sT.n
            if c == 8:
                g.wait_ge(sT.h, kt8_free)
                sOW.inc(g.dma_start(owTs, r128(owT_d)), 16)
        tap("yT", bigD[:, :])

    if PH >= 6:
        joinc(g); joind(g)
        sWQ.inc(g.dma_start(w1aTs[:, :, 1024:2048],
                            r128(w1aT_d)[:, :, 1024:2048]), 16)
        sWQ.inc(g.dma_start(w2aTs, r128(w2aT_d)), 16)
        t.wait_ge(sOW.h, sOW.n)
        joinc(t)
        ev4 = [0, 0, 0, 0]
        i = 0
        for m in range(ND):
            for nb in range(4):
                reg = psA[i % 4]
                if ev4[i % 4]:
                    t.wait_ge(sV.h, ev4[i % 4])
                last = None
                for kk in range(ND):
                    last = t.matmul(reg[:, :], owTs[:, kk, m * 128:(m + 1) * 128],
                                    yT[:, kk, nb * 512:(nb + 1) * 512],
                                    start=(kk == 0), stop=(kk == ND - 1))
                sT.inc(last)
                v.wait_ge(sT.h, sT.n)
                sV.inc(v.scalar_tensor_tensor(
                    x_aT[:, m, nb * 512:(nb + 1) * 512], reg[:, :],
                    P_obc[:, m:m + 1], ln1T[:, m, nb * 512:(nb + 1) * 512],
                    ALU.add, ALU.add))
                ev4[i % 4] = sV.n
                i += 1
            if PH >= 7:
                sc.wait_ge(sV.h, sV.n)
                v.wait_ge(sV.h, sV.n)
                gn_partials(x_aT, m)
        p4_mm_end = sT.n
        g.wait_ge(sWQ.h, sWQ.n)
        g.wait_ge(sT.h, p4_mm_end)
        sWQ.inc(g.dma_start(w1aTs[:, :, 0:1024], r128(w1aT_d)[:, :, 0:1024]),
                16)
        w12a_ld = sWQ.n
        tap("xaT", bigC[:, :])

    if PH >= 7:
        gn_stats_fold(x_aT, gn2i_d, gn2o_d, sC3, P_gn2g, P_gn2b, S_s2, S_b2,
                      B_b2bf, skip_partials=True)
        t.wait_ge(sS.h, sS.n)
        t.wait_ge(sWQ.h, w12a_ld)
        last = None
        for m in range(16):
            for kk in range(ND):
                last = t.matmul(psS[:, 32 + m:33 + m],
                                w1aTs[:, kk, m * 128:(m + 1) * 128],
                                B_b2bf[:, kk:kk + 1],
                                start=(kk == 0), stop=(kk == ND - 1))
        sT.inc(last)
        v.wait_ge(sT.h, sT.n)
        sV.inc(v.tensor_tensor(S_silu, psS[:, 32:48], P_b1ac, ALU.add))
        for kk in range(ND):
            v.tensor_scalar_mul(w1aTs[:, kk, :], w1aTs[:, kk, :], S_s2[:, kk:kk + 1])
        sV.inc(v.tensor_copy(bn6[:, 0:1], S_s2[:, 0:1]))
        w1a_scaled = sV.n
        tap("s2", S_s2); tap("b2", S_b2)

    xbrot = scrb[:, 0:2048]
    x0rot7 = scrb[:, 2048:6144].bitcast(F32).rearrange("p (a c) -> p a c", c=D)
    tmpf72 = [scrb[:, 6144:8192].bitcast(F32), scrb[:, 8192:10240].bitcast(F32)]
    ln2rot = scrb[:, 10240:11264]

    if PH >= 8:
        t.wait_ge(sV.h, w1a_scaled)
        sc.wait_ge(sV.h, w1a_scaled)
        xb_bounced = [0, 0, 0, 0]
        hps = [0, 0, 0, 0]
        ops = [0, 0]
        hs_read = 0
        p7 = dict(lnw=0, ln2_wr=0, x1_wr=0, ln2T_tb=[0, 0, 0, 0], first=True)

        def emit_p7_pair(pi):
            t0 = 2 * pi
            tb = t0 // 4
            if t0 % 4 == 0:
                sy.wait_ge(sXBt[tb].h, xb_bounced[tb])
            if p7["first"]:
                sy.wait_ge(sT.h, p4_mm_end)
                sc.wait_ge(sS.h, sS.n)
                p7["first"] = False
            if p7["lnw"]:
                sy.wait_ge(sV.h, p7["lnw"])
            for k in range(2):
                ti = t0 + k
                sFT[k].inc(sy.dma_start(xbrot[:, k * D:(k + 1) * D],
                                        xbscr[:, ti * 128:(ti + 1) * 128],
                                        transpose=True), 16)
            s0[0].inc(sy.dma_start(
                x0rot7, r128(x_nat[t0 * 128:(t0 + 2) * 128, :])), 16)
            for k in range(2):
                v.wait_ge(sFT[k].h, sFT[k].n)
                xbt = xbrot[:, k * D:(k + 1) * D]
                v.bn_stats(bn6[:, 12 * k:12 * k + 6], xbt[:, 0:512])
                ve(v.bn_stats(bn6[:, 12 * k + 6:12 * k + 12], xbt[:, 512:1024]))
                ve(v.bn_aggr(bn6[:, 48 + 2 * k:50 + 2 * k],
                             bn6[:, 12 * k:12 * k + 12]))
            sc.wait_ge(sV.h, sV.n)
            mv7 = bn6[:, 48:52].rearrange("p (a c) -> p a c", c=2)
            se(sc.activation(bn6[:, 56:58], mv7[:, :, 1], AF.Ln, bias=EPS))
            sS.inc(sc.activation(bn6[:, 56:58], bn6[:, 56:58], AF.Exp,
                                 scale=-0.5))
            v.wait_ge(sS.h, sS.n)
            if p7["x1_wr"]:
                v.wait_ge(sX1.h, p7["x1_wr"])
            for k in range(2):
                ti = t0 + k
                tmpk = tmpf72[k]
                ve(v.tensor_scalar(tmpk, xbrot[:, k * D:(k + 1) * D],
                                   bn6[:, 48 + 2 * k:49 + 2 * k],
                                   bn6[:, 56 + k:57 + k],
                                   ALU.subtract, ALU.mult))
                v.wait_ge(s0[0].h, s0[0].n)
                ve(v.tensor_tensor(tmpk, tmpk, x0rot7[:, k, :], ALU.add))
                g.wait_ge(sX1.h, sX1.n)
                g.wait_ge(sV.h, sV.n)
                sX1.inc(g.dma_start(x1scr[ti * 128:(ti + 1) * 128, :], tmpk),
                        16)
            p7["x1_wr"] = sX1.n
            p7["lnw"] = sV.n
            for k in range(2):
                v.bn_stats(bn6[:, 12 * k:12 * k + 6], tmpf72[k][:, 0:512])
                ve(v.bn_stats(bn6[:, 12 * k + 6:12 * k + 12],
                              tmpf72[k][:, 512:1024]))
                ve(v.bn_aggr(bn6[:, 52 + 2 * k:54 + 2 * k],
                             bn6[:, 12 * k:12 * k + 12]))
            sc.wait_ge(sV.h, sV.n)
            mv8 = bn6[:, 52:56].rearrange("p (a c) -> p a c", c=2)
            se(sc.activation(bn6[:, 58:60], mv8[:, :, 1], AF.Ln, bias=EPS))
            sS.inc(sc.activation(bn6[:, 58:60], bn6[:, 58:60], AF.Exp,
                                 scale=-0.5))
            v.wait_ge(sS.h, sS.n)
            for k in range(2):
                ti = t0 + k
                if p7["ln2_wr"]:
                    v.wait_ge(sLW[0].h, p7["ln2_wr"])
                sV.inc(v.tensor_scalar(ln2rot, tmpf72[k],
                                       bn6[:, 52 + 2 * k:53 + 2 * k],
                                       bn6[:, 58 + k:59 + k],
                                       ALU.subtract, ALU.mult))
                g.wait_ge(sLW[0].h, sLW[0].n)
                g.wait_ge(sV.h, sV.n)
                sLW[0].inc(g.dma_start(ln2scr[ti * 128:(ti + 1) * 128, :],
                                       ln2rot), 16)
                p7["ln2_wr"] = sLW[0].n
        def emit_p7_ln2T(tb):
            sy.wait_ge(sLW[0].h, sLW[0].n)
            for j in range(ND):
                sXBt[tb].inc(sy.dma_start(
                    ln2T[:, j, tb * 512:(tb + 1) * 512],
                    ln2scr[tb * 512:(tb + 1) * 512, j * 128:(j + 1) * 128],
                    transpose=True), 16)
            p7["ln2T_tb"][tb] = sXBt[tb].n

        xa3 = bigA[:, 12288:16384].rearrange("p (a c) -> p a c", c=512)
        for tb in range(4):
            if tb == 3:
                sV.inc(v.tensor_copy(xa3, x_aT[:, :, 1536:2048]))
                xa3_cp = sV.n
            for mh in range(16):
                reg = psA[mh % 4]
                if hps[mh % 4]:
                    t.wait_ge(sS.h, hps[mh % 4])
                last = None
                for kk in range(ND):
                    last = t.matmul(reg[:, :], w1aTs[:, kk, mh * 128:(mh + 1) * 128],
                                    x_aT[:, kk, tb * 512:(tb + 1) * 512],
                                    start=(kk == 0), stop=(kk == ND - 1))
                sT.inc(last)
                sc.wait_ge(sT.h, sT.n)
                if tb and mh == 0:
                    sc.wait_ge(sT.h, hs_read)
                sS.inc(sc.activation(hA[:, mh, :], reg[:, :], AF.Silu,
                                     bias=S_silu[:, mh:mh + 1]))
                hps[mh % 4] = sS.n
                if mh == 15:
                    h_ready = sS.n
                if PH >= 9 and tb >= 1 and mh == 9:
                    emit_p7_pair(2 * (tb - 1))
            if tb == 3:
                p6_hmm_end = sT.n
            t.wait_ge(sS.h, h_ready)
            for md in range(ND):
                reg = psB[md % 2]
                if ops[md % 2]:
                    t.wait_ge(sS.h, ops[md % 2])
                for kk in range(16):
                    t.matmul(reg[:, :], w2aTs[:, kk, md * 128:(md + 1) * 128],
                             hA[:, kk, :], start=(kk == 0), stop=False)
                xsrc = (xa3[:, md, :] if tb == 3
                        else x_aT[:, md, tb * 512:(tb + 1) * 512])
                last = t.matmul(reg[:, :], ident_s.ap(), xsrc,
                                start=False, stop=True)
                sT.inc(last)
                sc.wait_ge(sT.h, sT.n)
                sS.inc(sc.activation(x_bT[:, md, tb * 512:(tb + 1) * 512],
                                     reg[:, :], AF.Identity,
                                     bias=P_b2ac[:, md:md + 1]))
                ops[md % 2] = sS.n
                if PH >= 9 and tb >= 1 and md == 2:
                    emit_p7_pair(2 * (tb - 1) + 1)
            hs_read = sT.n
            sc.wait_ge(sS.h, sS.n)
            sXBt[tb].inc(sc.dma_start(
                r128(xbscr[:, tb * 512:(tb + 1) * 512]),
                x_bT[:, :, tb * 512:(tb + 1) * 512]), 16)
            xb_bounced[tb] = sXBt[tb].n
            if PH >= 9 and tb >= 1:
                emit_p7_ln2T(tb - 1)
        p6_v_end = sV.n
        p6_mm_end = sT.n
        tap("xbT", bigD[:, :])
        if PH >= 9:
            ln2T_tb = p7["ln2T_tb"]
            if PH < 10:
                emit_p7_pair(6)
                emit_p7_pair(7)
                emit_p7_ln2T(3)
            tap("ln2T", bigD[:, :])

    ffrot4 = bigC[:, 0:4096].rearrange("p (a c) -> p a c", c=D)
    tmpf9 = bigC[:, 4096:6144].bitcast(F32)
    outrot4 = bigC[:, 8192:16384].bitcast(F32).rearrange("p (a c) -> p a c", c=D)
    w1f_bufs = [bigC[:, 0:8192].rearrange("p (a c) -> p a c", c=D),
                scrb[:, 0:8192].rearrange("p (a c) -> p a c", c=D)]
    w2f_bufs = [bigC[:, 8192:16384].rearrange("p (a c) -> p a c", c=D),
                bigA[:, 0:8192].rearrange("p (a c) -> p a c", c=D)]

    if PH >= 10:
        g.wait_ge(sV.h, xa3_cp)
        g.wait_ge(sT.h, p6_hmm_end)
        joinc(t)
        ffo_bounced = [0, 0, 0, 0]
        hq_ps = [0, 0, 0, 0]
        oq_s = [0, 0]
        oq_v = [0, 0]
        p9 = dict(ffread=0, o_dma=(0, 0), ld=([0, 0, 0, 0], 0), first=True)
        qd_mm_end = [0, 0, 0, 0]
        wq1_ld = [[(0, 0)] * ND for _ in range(4)]
        wq2_ld = [0, 0, 0, 0]
        g.wait_ge(sC4.h, 3)

        def load_quarter(qd):
            pb8 = qd % 2
            if qd >= 2:
                g.wait_ge(sT.h, qd_mm_end[qd - 2])
            for mh in range(ND):
                s_ = (sWQ, sWQb)[mh % 2]
                g.wait_ge(s_.h, s_.n)
                s_.inc(g.dma_start(
                    w1f_bufs[pb8][:, :, mh * 128:(mh + 1) * 128],
                    r128(sg1g_d[:, qd * 1024 + mh * 128:
                                qd * 1024 + (mh + 1) * 128])), 16)
                wq1_ld[qd][mh] = (mh % 2, s_.n)
            g.wait_ge(sWQ.h, sWQ.n)
            sWQ.inc(g.dma_start(
                w2f_bufs[pb8], r128(sg2g_d[qd * 1024:(qd + 1) * 1024, :])), 16)
            wq2_ld[qd] = sWQ.n

        def emit_p9_ldA(tb):
            sy.wait_ge(sXBt[tb].h, ffo_bounced[tb])
            if p9["first"]:
                sy.wait_ge(sV.h, p7["v_end"])
                sy.wait_ge(sS.h, p7["s_end"])
                sy.wait_ge(sX1.h, sX1.n)
                sy.wait_ge(sT.h, qd_mm_end[2])
                p9["first"] = False
            if p9["ffread"]:
                sy.wait_ge(sV.h, p9["ffread"])
            trld = [0, 0, 0, 0]
            tsem = (sFT[0], sFT[1], sKS, sVS)
            for i4 in range(4):
                ti = 4 * tb + i4
                tsem[i4].inc(sy.dma_start(ffrot4[:, i4, :],
                                          ffoscr[:, ti * 128:(ti + 1) * 128],
                                          transpose=True), 16)
                trld[i4] = tsem[i4].n
            if p9["o_dma"][0]:
                sy.wait_ge(sOT[1].h, p9["o_dma"][0])
                sy.wait_ge(sOT2.h, p9["o_dma"][1])
            sOT[0].inc(sy.dma_start(
                outrot4, r128(x1scr[tb * 512:(tb + 1) * 512, :])), 16)
            p9["ld"] = (trld, sOT[0].n)

        def emit_p9_prcB(tb):
            tsem = (sFT[0], sFT[1], sKS, sVS)
            for i4 in range(4):
                v.wait_ge(tsem[i4].h, p9["ld"][0][i4])
                v.bn_stats(bn6[:, 12 * i4:12 * i4 + 6], ffrot4[:, i4, 0:512])
                ve(v.bn_stats(bn6[:, 12 * i4 + 6:12 * i4 + 12],
                              ffrot4[:, i4, 512:1024]))
                ve(v.bn_aggr(bn6[:, 48 + 2 * i4:50 + 2 * i4],
                             bn6[:, 12 * i4:12 * i4 + 12]))
            sc.wait_ge(sV.h, sV.n)
            mv9 = bn6[:, 48:56].rearrange("p (a c) -> p a c", c=2)
            se(sc.activation(bn6[:, 56:60], mv9[:, :, 1], AF.Ln, bias=EPS))
            sS.inc(sc.activation(bn6[:, 56:60], bn6[:, 56:60], AF.Exp,
                                 scale=-0.5))
            v.wait_ge(sS.h, sS.n)
            v.wait_ge(sOT[0].h, p9["ld"][1])
            for i4 in range(4):
                ti = 4 * tb + i4
                ve(v.tensor_scalar(tmpf9, ffrot4[:, i4, :],
                                   bn6[:, 48 + 2 * i4:49 + 2 * i4],
                                   bn6[:, 56 + i4:57 + i4],
                                   ALU.subtract, ALU.mult))
                ve(v.tensor_tensor(outrot4[:, i4, :], tmpf9,
                                   outrot4[:, i4, :], ALU.add))
                sy.wait_ge(sV.h, sV.n)
                (sOT[1], sOT2)[i4 % 2].inc(
                    sy.dma_start(out_d[ti * 128:(ti + 1) * 128, :],
                                 outrot4[:, i4, :]), 16)
            p9["ffread"] = sV.n
            p9["o_dma"] = (sOT[1].n, sOT2.n)

        load_quarter(0)
        for qd in range(4):
            w1fq = w1f_bufs[qd % 2]
            w2fq = w2f_bufs[qd % 2]
            if 1 <= qd < 3:
                load_quarter(qd + 1)
            if qd == 0:
                sc.wait_ge(sV.h, alpha_done)
                v.wait_ge(sV.h, alpha_done)
            for tb in range(4):
                if qd == 0:
                    t.wait_ge(sXBt[tb].h, ln2T_tb[tb])
                for mh in range(ND):
                    reg = psA[mh % 4]
                    if tb == 0:
                        wsem, wcnt = wq1_ld[qd][mh]
                        t.wait_ge((sWQ, sWQb)[wsem].h, wcnt)
                    if hq_ps[mh % 4]:
                        t.wait_ge(sS.h, hq_ps[mh % 4])
                    last = None
                    for kk in range(ND):
                        last = t.matmul(reg[:, :], w1fq[:, kk, mh * 128:(mh + 1) * 128],
                                        ln2T[:, kk, tb * 512:(tb + 1) * 512],
                                        start=(kk == 0), stop=(kk == ND - 1))
                    sT.inc(last)
                    sc.wait_ge(sT.h, sT.n)
                    sS.inc(sc.activation(hq[:, mh, :], reg[:, :], AF.Gelu,
                                         scale=S_a1[:, qd * 8 + mh:qd * 8 + mh + 1],
                                         bias=P_b1fc[:, qd * 8 + mh:qd * 8 + mh + 1]))
                    hq_ps[mh % 4] = sS.n
                    if mh == ND - 1:
                        hq_ready = sS.n
                    if qd == 0 and tb == 0 and mh == 1:
                        emit_p7_pair(6)
                    if PH >= 11 and qd == 3 and tb >= 1 and mh == 1:
                        emit_p9_ldA(tb - 1)
                t.wait_ge(sS.h, hq_ready)
                if tb == 0:
                    t.wait_ge(sWQ.h, wq2_ld[qd])
                for md in range(ND):
                    reg = psB[md % 2]
                    if oq_s[md % 2]:
                        t.wait_ge(sS.h, oq_s[md % 2])
                    if oq_v[md % 2]:
                        t.wait_ge(sV.h, oq_v[md % 2])
                    last = None
                    for kk in range(ND):
                        last = t.matmul(reg[:, :], w2fq[:, kk, md * 128:(md + 1) * 128],
                                        hq[:, kk, :], start=(kk == 0),
                                        stop=(kk == ND - 1 and qd != 3))
                    dst = ffoT[:, md, tb * 512:(tb + 1) * 512]
                    if qd == 3:
                        last = t.matmul(reg[:, :], ident_s.ap(), dst,
                                        start=False, stop=True)
                    sT.inc(last)
                    if qd == 0:
                        sc.wait_ge(sT.h, sT.n)
                        sS.inc(sc.copy(dst, reg[:, :]))
                        oq_s[md % 2] = sS.n
                    elif qd < 3:
                        v.wait_ge(sT.h, sT.n)
                        sV.inc(v.tensor_tensor(dst, reg[:, :], dst, ALU.add))
                        oq_v[md % 2] = sV.n
                    else:
                        sc.wait_ge(sT.h, sT.n)
                        sS.inc(sc.activation(dst, reg[:, :], AF.Identity,
                                             scale=S_a2[:, md:md + 1],
                                             bias=P_b2fc[:, md:md + 1]))
                        oq_s[md % 2] = sS.n
                    if PH >= 11 and qd == 3 and tb >= 1 and md == 1:
                        emit_p9_prcB(tb - 1)
                    if qd == 0 and tb == 1 and md == 1:
                        emit_p7_pair(7)
                    if qd == 0 and tb == 1 and md == 4:
                        emit_p7_ln2T(3)
                qd_mm_end[qd] = sT.n
                if qd == 0 and tb == 1:
                    p7["v_end"] = sV.n
                    p7["s_end"] = sS.n
                    g.wait_ge(sV.h, p7["v_end"])
                    g.wait_ge(sS.h, p7["s_end"])
                    for s_ in (sFT[0], sFT[1], s0[0], s0[1], sLW[0], sX1):
                        if s_.n:
                            g.wait_ge(s_.h, s_.n)
                    load_quarter(1)
                if qd == 3:
                    sc.wait_ge(sS.h, sS.n)
                    sXBt[tb].inc(sc.dma_start(
                        r128(ffoscr[:, tb * 512:(tb + 1) * 512]),
                        ffoT[:, :, tb * 512:(tb + 1) * 512]), 16)
                    ffo_bounced[tb] = sXBt[tb].n
            v.wait_ge(sV.h, sV.n)
        tap("ffoT", bigE[:, :])
        if PH >= 11:
            emit_p9_ldA(3)
            emit_p9_prcB(3)

    if PH < 11:
        joinc(v, sy)
        zf = scrb[:, 0:2048].bitcast(F32)
        sV.inc(v.memset(zf, 0.0))
        sy.wait_ge(sV.h, sV.n)
        for ti in range(NT):
            sOT[ti % 2].inc(sy.dma_start(out_d[ti * 128:(ti + 1) * 128, :],
                                         zf), 16)

    for s in ALL_DMA_SEMS:
        if s.n:
            sy.wait_ge(s.h, s.n)
    return nc


def _stage_inputs(inputs):
    x = np.asarray(inputs["L0_attn_x"], np.float32).reshape(16384, 1024)
    tw = lambda w: np.ascontiguousarray(np.asarray(w, np.float32).T)
    shared = {
        "qwT": tw(inputs["L0_attn_qw"]), "kwT": tw(inputs["L0_attn_kw"]),
        "vwT": tw(inputs["L0_attn_vw"]), "owT": tw(inputs["L0_attn_ow"]),
        "w1aT": tw(inputs["L0_attn_w1"]), "w2aT": tw(inputs["L0_attn_w2"]),
        "gn1g": _col128(inputs["L0_attn_norm_g"]), "gn1b": _col128(inputs["L0_attn_norm_b"]),
        "gn2g": _col128(inputs["L0_attn_ffn_norm_g"]), "gn2b": _col128(inputs["L0_attn_ffn_norm_b"]),
        "obc": _col128(inputs["L0_attn_ob"]), "b1ac": _col128(inputs["L0_attn_b1"]),
        "b2ac": _col128(inputs["L0_attn_b2"]), "b1fc": _col128(inputs["L0_ffn_b1"]),
        "b2fc": _col128(inputs["L0_ffn_b2"]),
    }
    w1T = tw(inputs["L0_ffn_w1"]); w2T = tw(inputs["L0_ffn_w2"])
    w1n = np.ascontiguousarray(np.asarray(inputs["L0_ffn_w1"], np.float32))
    w2n = np.ascontiguousarray(np.asarray(inputs["L0_ffn_w2"], np.float32))
    maps = []
    for r in range(NCORES):
        m = dict(shared)
        m["x_nat"] = np.ascontiguousarray(x[r * T:(r + 1) * T])
        m["w1slab"] = np.ascontiguousarray(w1T[r * 128:(r + 1) * 128])
        m["w2slab"] = np.ascontiguousarray(w2T[r * 512:(r + 1) * 512])
        m["w1nslab"] = np.ascontiguousarray(w1n[r * 512:(r + 1) * 512])
        m["w2nslab"] = np.ascontiguousarray(w2n[r * 128:(r + 1) * 128])
        m.update(_host_consts(r))
        maps.append(m)
    return maps


def kernel(**inputs):
    maps = _stage_inputs(inputs)
    nc = build_nc()
    res = run_bass_kernel_spmd(nc, maps, core_ids=list(range(NCORES)))
    outs = [res.results[r]["out"] for r in range(NCORES)]
    full = np.concatenate(outs, axis=0).reshape(1, 16384, 1024).astype(np.float32)
    if _TAPS:
        kernel._tap_results = [{k: res.results[r].get("tap_" + k) for k in _TAPS}
                               for r in range(NCORES)]
    return full


# version 42
# speedup vs baseline: 1.0096x; 1.0096x over previous
import math

import numpy as np
import ml_dtypes

import concourse.bass as bass
import concourse.mybir as mybir
from concourse.bass_utils import run_bass_kernel_spmd

F32 = mybir.dt.float32
BF16 = mybir.dt.bfloat16
AF = mybir.ActivationFunctionType
ALU = mybir.AluOpType
AXX = mybir.AxisListType.X

NCORES = 8
T = 2048
D = 1024
ND = 8
NT = 16
CH = 128
H = 16
DH = 64
HA = 2048
HF = 4096
NTOK = 64 * 16384
EPS = 1e-5

_LAST_PHASE = 99
_TAPS = ()


def _gammas64():
    lin = np.linspace(np.float32(math.log(1 / 32)), np.float32(math.log(1 / 512)),
                      16, dtype=np.float32)
    return 1.0 - np.exp(lin.astype(np.float64))


def _headmap_const(vals_h):
    out = np.zeros((128, 512), np.float64)
    for h in range(H):
        out[64 * (h % 2):64 * (h % 2) + 64, (h // 2) * 64:(h // 2) * 64 + 64] = vals_h[h]
    return out


def _simcol(h):
    bank = 2 * (h % 2) + (h // 8)
    slot = (h // 2) % 4
    return bank * 512 + slot * 128


def _host_consts(rank):
    g = _gammas64()
    i = np.arange(CH)
    mask = np.zeros((128, H * CH), np.float64)
    for h in range(H):
        c0 = _simcol(h)
        mask[:, c0:c0 + CH] = np.where(
            i[None, :] >= i[:, None],
            g[h] ** np.maximum(i[None, :] - i[:, None], 0), 0.0)
    gqt = np.zeros((128, ND * CH), np.float64)
    for j in range(ND):
        for half in range(2):
            h = 2 * j + half
            gqt[64 * half:64 * half + 64, j * CH:(j + 1) * CH] = \
                (g[h] ** (i + 1.0))[None, :]
    dk = np.zeros((128, D), np.float64)
    for h in range(H):
        dk[:, h * DH:(h + 1) * DH] = (g[h] ** (127 - i.astype(np.float64)))[:, None]
    gc = _headmap_const(g ** 128.0)
    gd = np.zeros((7, 128, 512), np.float64)
    for j in range(7):
        if j < rank:
            gd[j] = _headmap_const(g ** (2048.0 * (rank - 1 - j)))
    bf = ml_dtypes.bfloat16
    return {
        "maskbf": mask.astype(bf),
        "gqt": gqt.astype(bf),
        "dkc": dk.astype(bf),
        "gc": gc.astype(np.float32),
        "gd": np.ascontiguousarray(gd.transpose(1, 0, 2)).astype(np.float32),
        "ident": np.eye(128, dtype=bf),
        "onesb_row": np.ones((1, 128), bf),
        "onesb_col": np.ones((128, 1), bf),
        "onesf_row": np.ones((1, 128), np.float32),
        "ghalf": np.repeat(np.eye(2, dtype=np.float32), 64, axis=0),
        "bhalf": np.ascontiguousarray(np.repeat(np.eye(2, dtype=np.float32), 64, axis=1)),
        "bhalf2": np.ascontiguousarray(
            np.repeat(np.eye(2, dtype=np.float32), 64, axis=1).reshape(1, 256)),
    }


def _col128(vec):
    v = np.asarray(vec, np.float32)
    k = v.size // 128
    return np.ascontiguousarray(v.reshape(k, 128).T)


_TAP_SHAPES = {
    "ln1T": ([128, ND * T], BF16),
    "s1": ([128, 8], F32), "b1": ([128, 8], F32),
    "crow": ([1, 2 * D], BF16), "cq": ([128, 8], F32),
    "kvc": ([128, 16384], BF16),
    "kscr": ([T, D], BF16), "vscr": ([T, D], BF16),
    "A": ([128, 1024], BF16), "P": ([128, 1024], BF16),
    "qT": ([128, ND * T], BF16),
    "yT": ([128, ND * T], BF16),
    "xaT": ([128, ND * T], BF16),
    "xbT": ([128, ND * T], BF16),
    "x1": ([128, NT * D], BF16),
    "ln2T": ([128, ND * T], BF16),
    "ffoT": ([128, ND * T], BF16),
    "s2": ([128, 8], F32), "b2": ([128, 8], F32),
    "a1": ([128, 32], F32), "a2": ([128, 8], F32),
}


def build_nc():
    nc = bass.Bass("TRN2", target_bir_lowering=False)
    PH = _LAST_PHASE
    taps = set(_TAPS)

    def din(name, shape, dt=F32):
        return nc.dram_tensor(name, list(shape), dt, kind="ExternalInput").ap()

    x_nat = din("x_nat", [T, D])
    qwT_d = din("qwT", [D, D]); kwT_d = din("kwT", [D, D]); vwT_d = din("vwT", [D, D])
    owT_d = din("owT", [D, D])
    w1aT_d = din("w1aT", [D, HA]); w2aT_d = din("w2aT", [HA, D])
    w1s_d = din("w1slab", [128, HF]); w2s_d = din("w2slab", [512, D])
    w1ns_d = din("w1nslab", [512, D]); w2ns_d = din("w2nslab", [128, HF])
    gn1g_d = din("gn1g", [128, 8]); gn1b_d = din("gn1b", [128, 8])
    gn2g_d = din("gn2g", [128, 8]); gn2b_d = din("gn2b", [128, 8])
    obc_d = din("obc", [128, 8]); b1ac_d = din("b1ac", [128, 16]); b2ac_d = din("b2ac", [128, 8])
    b1fc_d = din("b1fc", [128, 32]); b2fc_d = din("b2fc", [128, 8])
    maskbf_d = din("maskbf", [128, H * CH], BF16)
    gqt_d = din("gqt", [128, ND * CH], BF16)
    dkc_d = din("dkc", [128, D], BF16)
    gc_d = din("gc", [128, 512])
    gd_d = din("gd", [128, 7, 512])
    ident_d = din("ident", [128, 128], BF16)
    onesbr_d = din("onesb_row", [1, 128], BF16)
    onesbc_d = din("onesb_col", [128, 1], BF16)
    onesf_d = din("onesf_row", [1, 128])
    ghalf_d = din("ghalf", [128, 2])
    bhalf_d = din("bhalf", [2, 128])
    bhalf2_d = din("bhalf2", [1, 256])

    out_d = nc.dram_tensor("out", [T, D], F32, kind="ExternalOutput").ap()

    def dscr(name, shape, dt):
        return nc.dram_tensor(name, list(shape), dt, kind="Internal").ap()

    lnscr = dscr("lnscr", [T, D], BF16)
    kscr = dscr("kscr", [T, D], BF16)
    vscr = dscr("vscr", [T, D], BF16)
    xbscr = dscr("xbscr", [D, T], BF16)
    ln2scr = dscr("ln2scr", [T, D], BF16)
    x1scr = dscr("x1scr", [T, D], F32)
    ffoscr = dscr("ffoscr", [D, T], BF16)
    ab_d = nc.dram_tensor("abounce", [128, 512], F32, kind="Internal")
    ag_d = nc.dram_tensor("agout", [128 * NCORES, 512], F32, kind="Internal")
    gn1i_d = nc.dram_tensor("gn1i", [2, 16], F32, kind="Internal")
    gn1o_d = nc.dram_tensor("gn1o", [2, 16], F32, kind="Internal")
    g1i2_d = dscr("g1i2", [1, 32], F32)
    g1o2_d = dscr("g1o2", [1, 32], F32)
    sg1s_d = dscr("sg1s", [128, HF], BF16)
    sg2s_d = dscr("sg2s", [512, D], BF16)
    apk_d = dscr("apk", [128, 5], F32)
    sg1g_d = dscr("sg1g", [D, HF], BF16)
    sg2g_d = dscr("sg2g", [HF, D], BF16)
    apg_d = dscr("apg", [NCORES * 128, 5], F32)
    gn2i_d = nc.dram_tensor("gn2i", [2, 16], F32, kind="Internal")
    gn2o_d = nc.dram_tensor("gn2o", [2, 16], F32, kind="Internal")

    tap_d = {}
    for tname in taps:
        shp, dt = _TAP_SHAPES[tname]
        tap_d[tname] = nc.dram_tensor("tap_" + tname, list(shp), dt,
                                      kind="ExternalOutput").ap()

    def sb(name, shape, dt):
        return nc.alloc_sbuf_tensor(name, list(shape), dt)

    bigA = sb("bigA", [128, ND * T], BF16)
    bigB = sb("bigB", [128, ND * T], BF16)
    bigC = sb("bigC", [128, ND * T], BF16)
    bigD = sb("bigD", [128, ND * T], BF16)
    bigE = sb("bigE", [128, ND * T], BF16)
    scrb = sb("scrb", [128, 11264], BF16)
    apc = sb("apc", [128, 49], F32)
    maskb = sb("maskb", [128, H * CH], BF16)
    gqt_s = sb("gqt_s", [128, ND * CH], BF16)
    dk_s = sb("dk_s", [128, D], BF16)
    gc_s = sb("gc_s", [128, 512], F32)
    ones_s = sb("ones_s", [1, 128], BF16)
    onesc_s = sb("onesc_s", [128, 1], BF16)
    onesf_s = sb("onesf_s", [1, 128], F32)
    ghalf_s = sb("ghalf_s", [128, 2], F32)
    bhalf_s = sb("bhalf_s", [2, 128], F32)
    bhalf2_s = sb("bhalf2_s", [1, 256], F32)
    ident_s = sb("ident_s", [128, 128], BF16)
    bn6 = sb("bn6", [128, 60], F32)
    prm = sb("prm", [128, 104], F32)
    P_gn1g = prm[:, 0:8]; P_gn1b = prm[:, 8:16]
    P_gn2g = prm[:, 16:24]; P_gn2b = prm[:, 24:32]
    P_obc = prm[:, 32:40]; P_b1ac = prm[:, 40:56]
    P_b1fc = prm[:, 56:88]; P_b2ac = prm[:, 88:96]; P_b2fc = prm[:, 96:104]
    sml = sb("sml", [128, 104], F32)
    S_s1 = sml[:, 0:8]; S_b1 = sml[:, 8:16]
    S_s2 = sml[:, 16:24]; S_b2 = sml[:, 24:32]
    S_cq = sml[:, 32:40]; S_silu = sml[:, 40:56]
    S_a1 = sml[:, 56:88]; S_a2 = sml[:, 88:96]
    S_st = sml[:, 96:104]
    smlb = sb("smlb", [128, 16], BF16)
    B_b1bf = smlb[:, 0:8]; B_b2bf = smlb[:, 8:16]
    rows = sb("rows", [2, 576], F32)
    R_ssrow = rows[0:1, 0:512]; R_rrow = R_ssrow
    R_statloc = rows[0:2, 512:528]; R_statglob = rows[0:2, 528:544]
    R_musq = rows[0:2, 544:552]; R_var8 = rows[0:2, 552:560]
    R_bcin = rows[0:2, 560:576]
    crow = sb("crow", [1, 2 * D], BF16)
    C_k = crow[0:1, 0:D]; C_v = crow[0:1, D:2 * D]
    stt = sb("stt", [128, 1056], F32)
    A_s = stt[:, 0:512]; P_s = stt[:, 512:1024]
    part_s = stt[:, 1024:1040]
    bc_s = stt[:, 1040:1056]
    pbf = sb("pbf", [128, 512], BF16)
    rk16 = sb("rk16", [128, 16], F32)
    cpool = sb("cpool", [128, 8], F32)
    _CONSTS = [EPS, math.log(0.125), 1e-24, 1.0 / D, 1.0 / NTOK, 1.0 / HF, 1.0]

    psA = [nc.alloc_psum_tensor(f"psA{i}", [128, 512], F32) for i in range(4)]
    psB = [nc.alloc_psum_tensor(f"psB{i}", [128, 512], F32) for i in range(2)]
    psR = nc.alloc_psum_tensor("psR", [128, 512], F32)
    psS = nc.alloc_psum_tensor("psS", [128, 512], F32)

    class Sem:
        def __init__(self, name):
            self.h = nc.semaphore(name).__enter__()
            self.n = 0

        def inc(self, inst, k=1):
            inst.then_inc(self.h, k)
            self.n += k
            return self.n

        def incd(self, inst):
            inst.then_inc(self.h)
            self.n = max(self.n, 1)

    sT = Sem("sT"); sS = Sem("sS"); sV = Sem("sV"); sG = Sem("sG")
    sC1 = Sem("sC1"); sC2 = Sem("sC2"); sC3 = Sem("sC3"); sC4 = Sem("sC4")
    sLD = Sem("sLD"); sTR = Sem("sTR"); sXB = Sem("sXB"); sWQ = Sem("sWQ")
    sWG = Sem("sWG"); sOW = Sem("sOW"); sKS = Sem("sKS"); sVS = Sem("sVS")
    sTAP = Sem("sTAP"); sLDW = Sem("sLDW"); sAB = Sem("sAB")
    sX1 = Sem("sX1")
    s0 = [Sem("s0a"), Sem("s0b")]
    sXB4 = [Sem(f"sXB4_{i}") for i in range(8)]
    sXBt = [Sem(f"sXBt_{i}") for i in range(4)]
    sLW4 = [Sem(f"sLW4_{i}") for i in range(4)]
    sLW = [Sem("sLWa"), Sem("sLWb")]
    sFT = [Sem("sFTa"), Sem("sFTb")]
    sVC = [Sem("sVCa"), Sem("sVCb")]
    sKS2 = [Sem("sKS2a"), Sem("sKS2b")]
    sVS2 = [Sem("sVS2a"), Sem("sVS2b")]
    sAL = [Sem("sALa"), Sem("sALb")]
    sGN = Sem("sGN"); sWQb = Sem("sWQb"); sOT2 = Sem("sOT2")
    sAJ = [Sem("sAJa"), Sem("sAJb")]
    sOT = [Sem("sOTa"), Sem("sOTb")]
    ALL_DMA_SEMS = [sLD, sTR, sXB, sWQ, sWG, sOW, sKS, sVS, sTAP, sLDW, sAB, sX1,
                    *s0, *sLW, *sFT, *sVC, *sAL, *sAJ, *sOT, *sXB4, *sLW4,
                    *sXBt, *sKS2, *sVS2, sGN, sWQb, sOT2]

    g, sy, t, v, sc = nc.gpsimd, nc.sync, nc.tensor, nc.vector, nc.scalar

    def ve(inst):
        sV.inc(inst); v.wait_ge(sV.h, sV.n)

    def se(inst):
        sS.inc(inst); sc.wait_ge(sS.h, sS.n)

    def joinc(*engines):
        for e in engines:
            for s in (sT, sS, sV, sG):
                if s.n:
                    e.wait_ge(s.h, s.n)

    def joind(*engines):
        for e in engines:
            for s in ALL_DMA_SEMS:
                if s.n:
                    e.wait_ge(s.h, s.n)

    def r128(ap):
        return ap.rearrange("(a p) c -> p a c", p=128)

    def tap(name, src_ap):
        if name not in taps:
            return
        joinc(sy); joind(sy)
        sTAP.inc(sy.dma_start(tap_d[name], src_ap), 16)

    ln1T = bigA[:, :].rearrange("p (a c) -> p a c", c=T)
    kT = bigB[:, :].rearrange("p (a c) -> p a c", c=T)
    qT = bigC[:, :].rearrange("p (a c) -> p a c", c=T)
    yT = bigD[:, :].rearrange("p (a c) -> p a c", c=T)
    kvc = bigE[:, :].bitcast(F32).rearrange("p (a c) -> p a c", c=512)
    x_aT = qT
    x_bT = yT
    ln2T = yT
    ffoT = bigE[:, :].rearrange("p (a c) -> p a c", c=T)
    hA = bigA[:, 0:8192].rearrange("p (a c) -> p a c", c=512)
    hq = bigA[:, 8192:12288].rearrange("p (a c) -> p a c", c=512)
    w1aTs = bigB[:, :].rearrange("p (a c) -> p a c", c=HA)
    w2aTs = bigE[:, :].rearrange("p (a c) -> p a c", c=D)
    w1fq = bigC[:, 0:8192].rearrange("p (a c) -> p a c", c=D)
    w2fq = bigC[:, 8192:16384].rearrange("p (a c) -> p a c", c=D)
    kwTs = bigB[:, 0:8192].rearrange("p (a c) -> p a c", c=D)
    vwTs = bigB[:, 8192:16384].rearrange("p (a c) -> p a c", c=D)
    qwTs = bigD[:, 0:8192].rearrange("p (a c) -> p a c", c=D)

    for dst, src in ((maskb.ap(), maskbf_d), (gqt_s.ap(), gqt_d), (dk_s.ap(), dkc_d),
                     (gc_s.ap(), gc_d),
                     (ones_s.ap(), onesbr_d), (onesc_s.ap(), onesbc_d),
                     (onesf_s.ap(), onesf_d), (ghalf_s.ap(), ghalf_d),
                     (bhalf_s.ap(), bhalf_d), (bhalf2_s.ap(), bhalf2_d),
                     (ident_s.ap(), ident_d)):
        sLD.inc(sy.dma_start(dst, src), 16)
    for dst, src in ((prm[:, 0:8], gn1g_d), (prm[:, 8:16], gn1b_d),
                     (prm[:, 16:24], gn2g_d), (prm[:, 24:32], gn2b_d),
                     (prm[:, 32:40], obc_d), (prm[:, 40:56], b1ac_d),
                     (prm[:, 56:88], b1fc_d), (prm[:, 88:96], b2ac_d),
                     (prm[:, 96:104], b2fc_d)):
        sLD.inc(sy.dma_start(dst, src), 16)

    sG.inc(g.memset(A_s, 0.0))
    sG.inc(g.memset(P_s, 0.0))
    for ci, cv in enumerate(_CONSTS):
        sG.inc(g.memset(cpool[:, ci:ci + 1], cv))
        nc.const_aps.register(F32, cv, cpool[:, ci:ci + 1])
    consts_g = sG.n

    x0rot4 = scrb[:, 0:8192].rearrange("p (a c) -> p a c", c=D)
    junk1 = scrb[:, 8192:10240]

    def lnstats(src_ap, junk):
        se(sc.activation(junk, src_ap, AF.Identity, accum_out=S_st[:, 2:3]))
        sS.inc(sc.activation(junk, src_ap, AF.Square, accum_out=S_st[:, 3:4]))
        v.wait_ge(sS.h, sS.n)
        ve(v.tensor_scalar_mul(S_st[:, 0:1], S_st[:, 2:3], 1.0 / D))
        ve(v.tensor_scalar_mul(S_st[:, 4:5], S_st[:, 0:1], S_st[:, 0:1]))
        sV.inc(v.tensor_scalar(S_st[:, 1:2], S_st[:, 3:4], 1.0 / D, S_st[:, 4:5],
                               ALU.mult, ALU.subtract))
        sc.wait_ge(sV.h, sV.n)
        se(sc.activation(S_st[:, 6:7], S_st[:, 1:2], AF.Ln, bias=EPS))
        sS.inc(sc.activation(S_st[:, 5:6], S_st[:, 6:7], AF.Exp, scale=-0.5))
        v.wait_ge(sS.h, sS.n)

    def lnstats_v(src_ap):
        v.bn_stats(bn6[:, 0:6], src_ap[:, 0:512])
        ve(v.bn_stats(bn6[:, 6:12], src_ap[:, 512:1024]))
        sV.inc(v.bn_aggr(S_st[:, 0:2], bn6[:, 0:12]))
        sc.wait_ge(sV.h, sV.n)
        se(sc.activation(S_st[:, 6:7], S_st[:, 1:2], AF.Ln, bias=EPS))
        sS.inc(sc.activation(S_st[:, 5:6], S_st[:, 6:7], AF.Exp, scale=-0.5))
        v.wait_ge(sS.h, sS.n)

    sqf2 = [bigC[:, 0:2048].bitcast(F32), bigC[:, 2048:4096].bitcast(F32)]
    onesfc = cpool[:, 6:7]
    R2 = rows[0:1, 512:576]
    if PH >= 1:
        v.wait_ge(sG.h, consts_g)
        sc.wait_ge(sG.h, consts_g)
        t.wait_ge(sG.h, consts_g)
        t.wait_ge(sLD.h, sLD.n)
        tile_mm = [0] * 8
        tile_sq = [0] * 8
        sq_rd = [0, 0]
        tr_ev = [0, 0]
        prev_var = 0
        for gi in range(4):
            for i4 in range(4):
                ti = 4 * gi + i4
                i8 = ti % 8
                if tile_mm[i8]:
                    g.wait_ge(sT.h, tile_mm[i8])
                    g.wait_ge(sS.h, tile_sq[i8])
                sXB4[i8].inc(g.dma_start(x0rot4[:, i8, :],
                                         x_nat[ti * 128:(ti + 1) * 128, :]), 16)
            for i4 in range(4):
                ti = 4 * gi + i4
                i8 = ti % 8
                v.wait_ge(sXB4[i8].h, sXB4[i8].n)
                if i4 == 0 and prev_var:
                    v.wait_ge(sV.h, prev_var)
                v.bn_stats(bn6[:, 12 * i4:12 * i4 + 6], x0rot4[:, i8, 0:512])
                ve(v.bn_stats(bn6[:, 12 * i4 + 6:12 * i4 + 12],
                              x0rot4[:, i8, 512:1024]))
                ve(v.bn_aggr(bn6[:, 48 + 2 * i4:50 + 2 * i4],
                             bn6[:, 12 * i4:12 * i4 + 12]))
                sc.wait_ge(sV.h, sV.n)
                se(sc.activation(bn6[:, 56 + i4:57 + i4],
                                 bn6[:, 49 + 2 * i4:50 + 2 * i4], AF.Ln,
                                 bias=EPS))
                sS.inc(sc.activation(bn6[:, 56 + i4:57 + i4],
                                     bn6[:, 56 + i4:57 + i4], AF.Exp,
                                     scale=-0.5))
                v.wait_ge(sS.h, sS.n)
                sV.inc(v.tensor_scalar(x0rot4[:, i8, :], x0rot4[:, i8, :],
                                       bn6[:, 48 + 2 * i4:49 + 2 * i4],
                                       bn6[:, 56 + i4:57 + i4],
                                       ALU.subtract, ALU.mult))
                apl = sV.n
                sqs = sqf2[ti % 2]
                if sq_rd[ti % 2]:
                    sc.wait_ge(sT.h, sq_rd[ti % 2])
                sc.wait_ge(sV.h, apl)
                sS.inc(sc.activation(sqs, x0rot4[:, i8, :], AF.Square))
                tile_sq[i8] = sS.n
                t.wait_ge(sV.h, apl)
                for hb in range(2):
                    t.matmul(psA[hb][0:1, 0:512], onesc_s.ap(),
                             x0rot4[:, i8, hb * 512:(hb + 1) * 512],
                             start=(ti == 0), stop=(ti == 15))
                for half in range(2):
                    bank = psB[half]
                    if tr_ev[half]:
                        t.wait_ge(sS.h, tr_ev[half])
                    for j4 in range(4):
                        jb = half * 4 + j4
                        last = t.matmul(bank[:, j4 * 128:(j4 + 1) * 128],
                                        x0rot4[:, i8, jb * 128:(jb + 1) * 128],
                                        ident_s.ap(), start=True, stop=True)
                    sT.inc(last)
                    sc.wait_ge(sT.h, sT.n)
                    sS.inc(sc.copy(
                        ln1T[:, half * 4:half * 4 + 4, ti * 128:(ti + 1) * 128],
                        bank[:, 0:512]))
                    tr_ev[half] = sS.n
                tile_mm[i8] = sT.n
                t.wait_ge(sS.h, tile_sq[i8])
                last = None
                for hb in range(2):
                    last = t.matmul(psA[2 + hb][0:1, 0:512], onesfc,
                                    sqs[:, hb * 512:(hb + 1) * 512],
                                    start=(ti == 0), stop=(ti == 15))
                sT.inc(last)
                sq_rd[ti % 2] = sT.n
                if i4 == 3:
                    prev_var = sV.n
        v.wait_ge(sT.h, sT.n)
        last = None
        for st_, cb in ((0, 0), (2, 16)):
            for gg in range(16):
                bk = st_ + gg // 8
                col = cb + (gg % 2) * 8 + gg // 2
                last = v.tensor_reduce(R2[:, col:col + 1],
                                       psA[bk][0:1, (gg % 8) * 64:(gg % 8) * 64 + 64],
                                       AXX, ALU.add)
        sV.inc(last)
        grp_done = sV.n
        g.wait_ge(sV.h, grp_done)
        sWG.inc(g.dma_start(g1i2_d, R2[:, 0:32]), 16)
        g.wait_ge(sWG.h, sWG.n)
        sC1.incd(g.collective_compute(
            "AllReduce", ALU.add, replica_groups=[list(range(NCORES))],
            ins=[g1i2_d.opt()], outs=[g1o2_d.opt()]))
        sLDW.inc(g.dma_start(kwTs, r128(kwT_d)), 16)
        sLDW.inc(g.dma_start(vwTs, r128(vwT_d)), 16)
        sLDW.inc(g.dma_start(qwTs, r128(qwT_d)), 16)
        g.wait_ge(sC1.h, 1)
        sWG.inc(g.dma_start(R2[:, 0:32], g1o2_d), 16)
        gn1_ld = sWG.n
        ln1T_done = sS.n
        tap("ln1T", bigA[:, :])

    sqb2 = scrb[:, 8192:10240]

    def gn_partials(srcT, j):
        ve(v.tensor_reduce(part_s[:, j:j + 1], srcT[:, j, :], AXX, ALU.add))
        se(sc.activation(sqb2, srcT[:, j, :], AF.Square,
                         accum_out=part_s[:, 8 + j:9 + j]))

    def gn_stats_fold(srcT, gni_dram, gno_dram, ccsem, Pg, Pb, Ss, Sb, Bbf,
                      skip_partials=False):
        if not skip_partials:
            for j in range(ND):
                gn_partials(srcT, j)
        sV.inc(v.tensor_copy(bn6[:, 0:1], part_s[:, 0:1]))
        t.wait_ge(sV.h, sV.n)
        t.wait_ge(sS.h, sS.n)
        sT.inc(t.matmul(psR[0:2, 0:16], ghalf_s.ap(), part_s, start=True, stop=True))
        sc.wait_ge(sT.h, sT.n)
        sS.inc(sc.copy(R_statloc, psR[0:2, 0:16]))
        g.wait_ge(sS.h, sS.n)
        sWG.inc(g.dma_start(gni_dram.ap(), R_statloc), 16)
        g.wait_ge(sWG.h, sWG.n)
        ccsem.incd(g.collective_compute(
            "AllReduce", ALU.add, replica_groups=[list(range(NCORES))],
            ins=[gni_dram.ap().opt()], outs=[gno_dram.ap().opt()]))
        g.wait_ge(ccsem.h, 1)
        sWG.inc(g.dma_start(R_statglob, gno_dram.ap()), 16)
        v.wait_ge(sWG.h, sWG.n)
        ve(v.tensor_scalar_mul(R_statglob, R_statglob, 1.0 / NTOK))
        ve(v.tensor_tensor(R_musq, R_statglob[:, 0:8], R_statglob[:, 0:8], ALU.mult))
        ve(v.tensor_sub(R_var8, R_statglob[:, 8:16], R_musq))
        sV.inc(v.tensor_copy(R_bcin[:, 0:8], R_statglob[:, 0:8]))
        sc.wait_ge(sV.h, sV.n)
        se(sc.activation(R_bcin[:, 8:16], R_var8, AF.Ln, bias=EPS))
        sS.inc(sc.activation(R_bcin[:, 8:16], R_bcin[:, 8:16], AF.Exp, scale=-0.5))
        t.wait_ge(sS.h, sS.n)
        sT.inc(t.matmul(psR[0:128, 0:16], bhalf_s.ap(), R_bcin, start=True, stop=True))
        sc.wait_ge(sT.h, sT.n)
        sS.inc(sc.copy(bc_s, psR[0:128, 0:16]))
        v.wait_ge(sS.h, sS.n)
        ve(v.tensor_tensor(Ss, bc_s[:, 8:16], Pg, ALU.mult))
        ve(v.tensor_tensor(part_s[:, 0:8], bc_s[:, 0:8], Ss, ALU.mult))
        sV.inc(v.tensor_sub(Sb, Pb, part_s[:, 0:8]))
        sc.wait_ge(sV.h, sV.n)
        sS.inc(sc.copy(Bbf, Sb))

    stg1 = bigC[:, 0:4096]
    stg2f = bigC[:, 4096:8192]
    stg2 = stg2f.rearrange("p (a c) -> p a c", c=D)
    scrf = bigC[:, 8192:16384].bitcast(F32)
    if PH >= 2:
        g.wait_ge(sLDW.h, sLDW.n)
        g.wait_ge(sT.h, max(sq_rd))
        g.wait_ge(sS.h, max(tile_sq))
        sAL[0].inc(g.dma_start(stg1, w1s_d), 16)
        sAL[1].inc(g.dma_start(stg2, r128(w2s_d)), 16)
        stg_ld = (sAL[0].n, sAL[1].n)
        g.wait_ge(sAL[0].h, sAL[0].n)
        sAL[0].inc(g.dma_start(scrf.rearrange("p (a c) -> p a c", c=D),
                               r128(w1ns_d)), 16)
        w1n_ld = sAL[0].n
    if PH >= 2:
        v.wait_ge(sWG.h, gn1_ld)
        v.wait_ge(sLD.h, sLD.n)
        ve(v.tensor_scalar_mul(R2[:, 32:48], R2[:, 0:16], 1.0 / NTOK))
        ve(v.tensor_tensor(R2[:, 0:16], R2[:, 32:48], R2[:, 32:48], ALU.mult))
        ve(v.tensor_scalar_mul(R2[:, 16:32], R2[:, 16:32], 1.0 / NTOK))
        sV.inc(v.tensor_sub(R2[:, 16:32], R2[:, 16:32], R2[:, 0:16]))
        sc.wait_ge(sV.h, sV.n)
        se(sc.activation(R2[:, 48:64], R2[:, 16:32], AF.Ln, bias=EPS))
        sS.inc(sc.activation(R2[:, 48:64], R2[:, 48:64], AF.Exp, scale=-0.5))
        t.wait_ge(sS.h, sS.n)
        t.wait_ge(sLD.h, sLD.n)
        t.matmul(psR[0:128, 0:8], bhalf2_s.ap()[:, 0:128], R2[:, 32:40],
                 start=True, stop=False)
        t.matmul(psR[0:128, 0:8], bhalf2_s.ap()[:, 128:256], R2[:, 40:48],
                 start=False, stop=True)
        t.matmul(psR[0:128, 8:16], bhalf2_s.ap()[:, 0:128], R2[:, 48:56],
                 start=True, stop=False)
        sT.inc(t.matmul(psR[0:128, 8:16], bhalf2_s.ap()[:, 128:256],
                        R2[:, 56:64], start=False, stop=True))
        sc.wait_ge(sT.h, sT.n)
        sS.inc(sc.copy(bc_s, psR[0:128, 0:16]))
        v.wait_ge(sS.h, sS.n)
        ve(v.tensor_tensor(S_s1, bc_s[:, 8:16], P_gn1g, ALU.mult))
        ve(v.tensor_tensor(part_s[:, 0:8], bc_s[:, 0:8], S_s1, ALU.mult))
        sV.inc(v.tensor_sub(S_b1, P_gn1b, part_s[:, 0:8]))
        sc.wait_ge(sV.h, sV.n)
        sS.inc(sc.copy(B_b1bf, S_b1))
        v.wait_ge(sLDW.h, sLDW.n)
        t.wait_ge(sS.h, sS.n)
        t.wait_ge(sLD.h, sLD.n)
        t.wait_ge(sLDW.h, sLDW.n)
        for wsrc, cdst in ((kwTs, C_k), (vwTs, C_v)):
            t.wait_ge(sS.h, sS.n)
            last = None
            for nb2 in range(2):
                reg = psS[32 * nb2:32 * nb2 + 1, 0:512]
                for kk in range(ND):
                    last = t.matmul(reg, B_b1bf[:, kk:kk + 1],
                                    wsrc[:, kk, nb2 * 512:(nb2 + 1) * 512],
                                    start=(kk == 0), stop=(kk == ND - 1),
                                    tile_position=(0, 32 * nb2))
            sT.inc(last)
            sc.wait_ge(sT.h, sT.n)
            sS.inc(sc.copy(cdst[0:1, 0:512], psS[0:1, 0:512]))
            sS.inc(sc.copy(cdst[0:1, 512:1024], psS[32:33, 0:512]))
        last = None
        for m in range(ND):
            for kk in range(ND):
                last = t.matmul(psR[:, 16 + m:17 + m],
                                qwTs[:, kk, m * 128:(m + 1) * 128],
                                B_b1bf[:, kk:kk + 1],
                                start=(kk == 0), stop=(kk == ND - 1))
        sT.inc(last)
        sc.wait_ge(sT.h, sT.n)
        sS.inc(sc.copy(S_cq, psR[:, 16:24]))
        v.wait_ge(sT.h, sT.n)
        for w in (qwTs, kwTs, vwTs):
            for kk in range(ND):
                v.tensor_scalar_mul(w[:, kk, :], w[:, kk, :], S_s1[:, kk:kk + 1])
        sV.inc(v.tensor_copy(bn6[:, 0:1], S_s1[:, 0:1]))
        wscale_v = sV.n
        tap("s1", S_s1); tap("b1", S_b1); tap("crow", crow.ap()); tap("cq", S_cq)

        sc.wait_ge(sAL[0].h, stg_ld[0])
        sc.wait_ge(sAL[1].h, stg_ld[1])
        se(sc.sign(stg1, stg1))
        sS.inc(sc.sign(stg2f, stg2f))
        sc.wait_ge(sS.h, sS.n)
        sGN.inc(sc.dma_start(sg1s_d, stg1), 16)
        sGN.inc(sc.dma_start(r128(sg2s_d), stg2), 16)
        sc.wait_ge(sAL[0].h, w1n_ld)
        for a_ in range(4):
            se(sc.activation(scrf[:, a_ * D:(a_ + 1) * D],
                             scrf[:, a_ * D:(a_ + 1) * D], AF.Abs,
                             accum_out=apc[:, a_:a_ + 1]))
        g.wait_ge(sS.h, sS.n)
        g.wait_ge(sAL[1].h, sAL[1].n)
        sAL[1].inc(g.dma_start(scrf, w2ns_d), 16)
        sc.wait_ge(sAL[1].h, sAL[1].n)
        for a_ in range(4):
            se(sc.activation(scrf[:, a_ * D:(a_ + 1) * D],
                             scrf[:, a_ * D:(a_ + 1) * D], AF.Abs,
                             accum_out=apc[:, 5 + a_:6 + a_]))
        v.wait_ge(sS.h, sS.n)
        sV.inc(v.tensor_reduce(apc[:, 4:5], apc[:, 5:9], AXX, ALU.add))
        sc.wait_ge(sV.h, sV.n)
        sGN.inc(sc.dma_start(apk_d, apc[:, 0:5]), 16)
        sgn_stores = sGN.n
        g.wait_ge(sGN.h, sgn_stores)
        for cin, cout in ((apk_d, apg_d), (sg1s_d, sg1g_d), (sg2s_d, sg2g_d)):
            sC4.incd(g.collective_compute(
                "AllGather", ALU.bypass, replica_groups=[list(range(NCORES))],
                ins=[cin.opt()], outs=[cout.opt()]))

    kraw2 = [scrb[:, 0:1024], scrb[:, 1024:2048]]
    k2b2 = [scrb[:, 2048:3072], scrb[:, 3072:4096]]
    vbf2 = [scrb[:, 4096:5120], scrb[:, 5120:6144]]
    jnk3a = scrb[:, 6144:7168]

    if PH >= 3:
        t.wait_ge(sV.h, wscale_v)
        t.wait_ge(sV.h, grp_done)
        t.wait_ge(sS.h, ln1T_done)
        t.wait_ge(sS.h, sS.n)
        evk = [0, 0, 0, 0]
        evv = [0, 0]
        kv_evac = 0
        kslot_free = [0, 0]
        kslot_dma = [0, 0]
        k2_read = [0, 0]
        vslot_read = [0, 0]
        vslot_dma = [0, 0]
        kv_read_v = 0
        for c in range(NT):
            sl = c % 2
            kraw = kraw2[sl]
            k2b = k2b2[sl]
            vbf = vbf2[sl]
            bk = 2 * sl
            last = None
            for nb2 in range(2):
                if evk[bk + nb2]:
                    t.wait_ge(sS.h, evk[bk + nb2])
                reg = psA[bk + nb2]
                for kk in range(ND):
                    t.matmul(reg[:, :], ln1T[:, kk, c * 128:(c + 1) * 128],
                             kwTs[:, kk, nb2 * 512:(nb2 + 1) * 512],
                             start=(kk == 0), stop=False)
                last = t.matmul(reg[:, :], ones_s.ap(),
                                C_k[0:1, nb2 * 512:(nb2 + 1) * 512],
                                start=False, stop=True)
            sT.inc(last)
            kmm = sT.n
            for nb2 in range(2):
                reg = psB[1] if nb2 == 0 else psR
                if evv[nb2]:
                    t.wait_ge(sS.h, evv[nb2])
                for kk in range(ND):
                    t.matmul(reg[:, :], ln1T[:, kk, c * 128:(c + 1) * 128],
                             vwTs[:, kk, nb2 * 512:(nb2 + 1) * 512],
                             start=(kk == 0), stop=False)
                last = t.matmul(reg[:, :], ones_s.ap(),
                                C_v[0:1, nb2 * 512:(nb2 + 1) * 512],
                                start=False, stop=True)
            sT.inc(last)
            vmm = sT.n
            sc.wait_ge(sT.h, kmm)
            if kslot_free[sl]:
                sc.wait_ge(sV.h, kslot_free[sl])
                sc.wait_ge(sKS2[sl].h, kslot_dma[sl])
            sc.copy(kraw[:, 0:512], psA[bk][:, :])
            se(sc.copy(kraw[:, 512:1024], psA[bk + 1][:, :]))
            evk[bk] = evk[bk + 1] = sS.n
            se(sc.activation(jnk3a, kraw, AF.Square, accum_out=S_st[:, 6:7]))
            se(sc.activation(S_st[:, 7:8], S_st[:, 6:7], AF.Ln, bias=1e-24))
            sS.inc(sc.activation(rk16[:, c:c + 1], S_st[:, 7:8], AF.Exp,
                                 scale=-0.5))
            k_done = sS.n
            sy.wait_ge(sS.h, k_done)
            sKS2[sl].inc(sy.dma_start(kscr[c * 128:(c + 1) * 128, :], kraw), 16)
            kslot_dma[sl] = sKS2[sl].n
            sc.wait_ge(sT.h, vmm)
            if vslot_read[sl]:
                sc.wait_ge(sT.h, vslot_read[sl])
                sc.wait_ge(sVS2[sl].h, vslot_dma[sl])
            sc.copy(vbf[:, 0:512], psB[1][:, :])
            sS.inc(sc.copy(vbf[:, 512:1024], psR[:, :]))
            evv[0] = evv[1] = sS.n
            vbf_done = sS.n
            sy.wait_ge(sS.h, vbf_done)
            sVS2[sl].inc(sy.dma_start(vscr[c * 128:(c + 1) * 128, :], vbf), 16)
            vslot_dma[sl] = sVS2[sl].n
            v.wait_ge(sS.h, k_done)
            if k2_read[sl]:
                v.wait_ge(sT.h, k2_read[sl])
            sV.inc(v.scalar_tensor_tensor(k2b, kraw, rk16[:, c:c + 1],
                                          dk_s.ap(), ALU.mult, ALU.mult))
            k2_done = sV.n
            kslot_free[sl] = sV.n
            t.wait_ge(sV.h, k2_done)
            t.wait_ge(sS.h, vbf_done)
            if kv_evac:
                t.wait_ge(sS.h, kv_evac)
            last = None
            for h in range(H):
                pb = 64 * (h % 2)
                last = t.matmul(psB[0][pb:pb + 64, (h // 2) * 64:(h // 2) * 64 + 64],
                                k2b[:, h * 64:(h + 1) * 64],
                                vbf[:, h * 64:(h + 1) * 64],
                                start=True, stop=True, tile_position=(0, pb))
            sT.inc(last)
            k2_read[sl] = sT.n
            vslot_read[sl] = sT.n
            sc.wait_ge(sT.h, sT.n)
            if kv_read_v:
                sc.wait_ge(sV.h, kv_read_v)
            sS.inc(sc.copy(kvc[:, c, :], psB[0][:, :]))
            kv_evac = sS.n
            v.wait_ge(sS.h, kv_evac)
            ve(v.tensor_tensor(A_s, A_s, gc_s.ap(), ALU.mult))
            ve(v.tensor_tensor(A_s, A_s, kvc[:, c, :], ALU.add))
            kv_read_v = sV.n
        p3a_mm_end = sT.n
        a_done = sV.n
        sy.wait_ge(sC4.h, 1)
        apl = apc[:, 9:49].rearrange("p (r a) -> p r a", a=5)
        sGN.inc(sy.dma_start(apl, r128(apg_d)), 16)
        v.wait_ge(sGN.h, sGN.n)
        ve(v.tensor_scalar_mul(
            S_a1.rearrange("p (r a) -> p r a", a=4), apl[:, :, 0:4], 1.0 / D))
        ve(v.tensor_scalar_mul(S_a2, apl[:, :, 4], 1.0 / HF))
        alpha_done = sV.n
        tap("a1", S_a1); tap("a2", S_a2)
        tap("kscr", kscr); tap("vscr", vscr)
        tap("kvc", bigE[:, 0:16384])
        tap("A", stt[:, 0:512].bitcast(BF16))

        sy.wait_ge(sV.h, a_done)
        sAB.inc(sy.dma_start(ab_d.ap(), A_s), 16)
        g.wait_ge(sAB.h, sAB.n)
        sC2.incd(g.collective_compute(
            "AllGather", ALU.bypass, replica_groups=[list(range(NCORES))],
            ins=[ab_d.ap().opt()], outs=[ag_d.ap().opt()]))

    qtmp = scrb[:, 0:4096].rearrange("p (a c) -> p a c", c=512)
    sqb8 = scrb[:, 4096:8192].rearrange("p (a c) -> p a c", c=512)
    repf = scrb[:, 8192:9216].bitcast(F32)

    if PH >= 4:
        joinc(t, sc, v)
        joind(t, sc, v)
        evq = [0, 0, 0, 0]
        ssrow_v = 0
        rep_evac = 0
        qtmp_scaled = 0
        prev_ssmm = 0
        gd_s = bigD[:, 8192:15360].bitcast(F32)
        ajrot = scrb[:, 9216:11264].bitcast(F32)
        p_init_done = 0
        for nb in range(4):
            sq_v = [0] * ND
            for m in range(ND):
                reg = psA[m % 4]
                if evq[m % 4]:
                    t.wait_ge(sS.h, evq[m % 4])
                last = None
                for kk in range(ND):
                    last = t.matmul(reg[:, :], qwTs[:, kk, m * 128:(m + 1) * 128],
                                    ln1T[:, kk, nb * 512:(nb + 1) * 512],
                                    start=(kk == 0), stop=(kk == ND - 1))
                sT.inc(last)
                sc.wait_ge(sT.h, sT.n)
                if qtmp_scaled:
                    sc.wait_ge(sV.h, qtmp_scaled)
                sS.inc(sc.activation(qtmp[:, m, :], reg[:, :], AF.Identity,
                                     bias=S_cq[:, m:m + 1]))
                evq[m % 4] = sS.n
                v.wait_ge(sS.h, sS.n)
                if m == 0 and prev_ssmm:
                    v.wait_ge(sT.h, prev_ssmm)
                sV.inc(v.tensor_tensor(sqb8[:, m, :], qtmp[:, m, :],
                                       qtmp[:, m, :], ALU.mult))
                sq_v[m] = sV.n
            if ssrow_v:
                t.wait_ge(sV.h, ssrow_v)
            for m in range(ND):
                t.wait_ge(sV.h, sq_v[m])
                sT.inc(t.matmul(psR[0:1, 0:512], onesc_s.ap(), sqb8[:, m, :],
                                start=(m == 0), stop=(m == ND - 1)))
            prev_ssmm = sT.n
            v.wait_ge(sT.h, sT.n)
            ve(v.tensor_scalar_max(R_ssrow, psR[0:1, 0:512], 1e-24))
            ssrow_v = sV.n
            sc.wait_ge(sV.h, sV.n)
            se(sc.activation(R_rrow, R_ssrow, AF.Ln))
            sS.inc(sc.activation(R_rrow, R_rrow, AF.Exp, scale=-0.5,
                                 bias=math.log(0.125)))
            t.wait_ge(sS.h, sS.n)
            if rep_evac:
                t.wait_ge(sS.h, rep_evac)
            sT.inc(t.matmul(psB[0][:, :], onesf_s.ap(), R_rrow, start=True, stop=True))
            sc.wait_ge(sT.h, sT.n)
            sS.inc(sc.copy(repf, psB[0][:, :]))
            rep_evac = sS.n
            v.wait_ge(sS.h, rep_evac)
            last = None
            for m in range(ND):
                last = v.tensor_tensor(qT[:, m, nb * 512:(nb + 1) * 512],
                                       qtmp[:, m, :], repf, ALU.mult)
            sV.inc(last)
            qtmp_scaled = sV.n
            if nb == 1:
                g.wait_ge(sC2.h, 1)
                sWG.inc(g.dma_start(gd_s[:, 0:3584],
                                    gd_d.rearrange("p a c -> p (a c)")), 16)
                gd_ld = sWG.n
                aj_use = [0, 0]
                for j in range(NCORES - 1):
                    sl = j % 2
                    ajs = ajrot[:, sl * 512:(sl + 1) * 512]
                    if aj_use[sl]:
                        g.wait_ge(sV.h, aj_use[sl])
                    sAJ[sl].inc(g.dma_start(
                        ajs, ag_d.ap()[j * 128:(j + 1) * 128, :]), 16)
                    v.wait_ge(sAJ[sl].h, sAJ[sl].n)
                    if j == 0:
                        v.wait_ge(sWG.h, gd_ld)
                    ve(v.tensor_tensor(ajs, ajs,
                                       gd_s[:, j * 512:(j + 1) * 512], ALU.mult))
                    ve(v.tensor_tensor(P_s, P_s, ajs, ALU.add))
                    aj_use[sl] = sV.n
                p_init_done = sV.n
        q_done = sV.n
        tap("qT", bigC[:, :])
        tap("P", stt[:, 512:1024].bitcast(BF16))

    owTs = bigB[:, 0:16384].rearrange("p (a c) -> p a c", c=T)[:, :, 0:1024]
    simbf = scrb[:, 0:2048]
    q2b = scrb[:, 2048:3072]
    vchunk = scrb[:, 3072:5120]

    if PH >= 5:
        sy.wait_ge(sKS2[0].h, sKS2[0].n)
        sy.wait_ge(sKS2[1].h, sKS2[1].n)
        sy.wait_ge(sT.h, p3a_mm_end)
        for j in range(ND):
            sTR.inc(sy.dma_start(kT[:, j, :], kscr[:, j * 128:(j + 1) * 128],
                                 transpose=True), 16)
        kT_ld = sTR.n
        joinc(t)
        q2v = q2b.rearrange("p (a c) -> p a c", c=128)
        gq3 = gqt_s.ap().rearrange("p (a c) -> p a c", c=128)

        sy.wait_ge(sVS2[0].h, sVS2[0].n)
        sy.wait_ge(sVS2[1].h, sVS2[1].n)
        sy.wait_ge(sV.h, q_done)
        sy.wait_ge(sT.h, prev_ssmm)
        sVC[0].inc(sy.dma_start(vchunk[:, 0:1024], vscr[0:128, :]), 16)
        vld = [sVC[0].n, 0]
        mask_done = [0, 0, 0, 0]
        yevac = [0, 0]
        p_up = p_init_done
        pbf_read = 0
        q2_done = 0
        vread = [0, 0]
        for c in range(NT):
            sl = c % 2
            if c + 1 < NT:
                if vread[1 - sl]:
                    sy.wait_ge(sT.h, vread[1 - sl])
                sVC[1 - sl].inc(sy.dma_start(
                    vchunk[:, (1 - sl) * 1024:(2 - sl) * 1024],
                    vscr[(c + 1) * 128:(c + 2) * 128, :]), 16)
                vld[1 - sl] = sVC[1 - sl].n
            if q2_done:
                v.wait_ge(sT.h, q2_done)
            if c == 0:
                v.wait_ge(sV.h, q_done)
                v.wait_ge(sT.h, p3a_mm_end)
            sV.inc(v.tensor_tensor(q2v, qT[:, :, c * 128:(c + 1) * 128], gq3,
                                   ALU.mult))
            q2r = sV.n
            sc.wait_ge(sV.h, p_up)
            if pbf_read:
                sc.wait_ge(sT.h, pbf_read)
            sS.inc(sc.activation(pbf.ap(), P_s, AF.Identity))
            pbf_s = sS.n
            t.wait_ge(sTR.h, kT_ld)
            for b in (0, 2, 1, 3):
                heads = [hh * 2 + (b // 2) for hh in range(4 * (b % 2), 4 * (b % 2) + 4)]
                pb = 64 * (b // 2)
                if mask_done[b]:
                    t.wait_ge(sV.h, mask_done[b])
                last = None
                for slot, h in enumerate(heads):
                    last = t.matmul(psA[b][:, slot * 128:(slot + 1) * 128],
                                    kT[pb:pb + 64, h // 2, c * 128:(c + 1) * 128],
                                    qT[pb:pb + 64, h // 2, c * 128:(c + 1) * 128],
                                    start=True, stop=True, tile_position=(pb, 0))
                sT.inc(last)
                v.wait_ge(sT.h, sT.n)
                sV.inc(v.scalar_tensor_tensor(
                    simbf[:, b * 512:(b + 1) * 512], psA[b][:, :],
                    rk16[:, c:c + 1], maskb[:, b * 512:(b + 1) * 512],
                    ALU.mult, ALU.mult))
                mask_done[b] = sV.n
            t.wait_ge(sV.h, q2r)
            t.wait_ge(sS.h, pbf_s)
            t.wait_ge(sVC[sl].h, vld[sl])
            for bank in range(2):
                if yevac[bank]:
                    t.wait_ge(sS.h, yevac[bank])
                t.wait_ge(sV.h, mask_done[2 + bank])
                last = None
                for hh in range(8):
                    h = 8 * bank + hh
                    pb = 64 * (h % 2)
                    j = h // 2
                    reg = psB[bank][pb:pb + 64, (j % 4) * 128:(j % 4) * 128 + 128]
                    sc0 = _simcol(h)
                    t.matmul(reg,
                             vchunk[:, sl * 1024 + h * 64:sl * 1024 + (h + 1) * 64],
                             simbf[:, sc0:sc0 + 128],
                             start=True, stop=False, tile_position=(0, pb))
                    last = t.matmul(reg, pbf[pb:pb + 64, j * 64:(j + 1) * 64],
                                    q2v[pb:pb + 64, j, :],
                                    start=False, stop=True, tile_position=(pb, pb))
                sT.inc(last)
                sc.wait_ge(sT.h, sT.n)
                ps3 = psB[bank][:, :].rearrange("p (a c) -> p a c", c=128)
                y3 = yT[:, 4 * bank:4 * bank + 4, c * 128:(c + 1) * 128]
                sS.inc(sc.activation(y3, ps3, AF.Identity))
                yevac[bank] = sS.n
            pbf_read = sT.n
            vread[sl] = sT.n
            q2_done = sT.n
            v.wait_ge(sS.h, pbf_s)
            ve(v.tensor_tensor(P_s, P_s, gc_s.ap(), ALU.mult))
            ve(v.tensor_tensor(P_s, P_s, kvc[:, c, :], ALU.add))
            p_up = sV.n
            if c == 7:
                kt8_free = sT.n
            if c == 8:
                g.wait_ge(sT.h, kt8_free)
                sOW.inc(g.dma_start(owTs, r128(owT_d)), 16)
        tap("yT", bigD[:, :])

    if PH >= 6:
        joinc(g); joind(g)
        sWQ.inc(g.dma_start(w1aTs[:, :, 1024:2048],
                            r128(w1aT_d)[:, :, 1024:2048]), 16)
        sWQ.inc(g.dma_start(w2aTs, r128(w2aT_d)), 16)
        t.wait_ge(sOW.h, sOW.n)
        joinc(t)
        ev4 = [0, 0, 0, 0]
        i = 0
        for m in range(ND):
            for nb in range(4):
                reg = psA[i % 4]
                if ev4[i % 4]:
                    t.wait_ge(sV.h, ev4[i % 4])
                last = None
                for kk in range(ND):
                    last = t.matmul(reg[:, :], owTs[:, kk, m * 128:(m + 1) * 128],
                                    yT[:, kk, nb * 512:(nb + 1) * 512],
                                    start=(kk == 0), stop=(kk == ND - 1))
                sT.inc(last)
                v.wait_ge(sT.h, sT.n)
                sV.inc(v.scalar_tensor_tensor(
                    x_aT[:, m, nb * 512:(nb + 1) * 512], reg[:, :],
                    P_obc[:, m:m + 1], ln1T[:, m, nb * 512:(nb + 1) * 512],
                    ALU.add, ALU.add))
                ev4[i % 4] = sV.n
                i += 1
            if PH >= 7:
                sc.wait_ge(sV.h, sV.n)
                v.wait_ge(sV.h, sV.n)
                gn_partials(x_aT, m)
        p4_mm_end = sT.n
        g.wait_ge(sWQ.h, sWQ.n)
        g.wait_ge(sT.h, p4_mm_end)
        sWQ.inc(g.dma_start(w1aTs[:, :, 0:1024], r128(w1aT_d)[:, :, 0:1024]),
                16)
        w12a_ld = sWQ.n
        tap("xaT", bigC[:, :])

    if PH >= 7:
        gn_stats_fold(x_aT, gn2i_d, gn2o_d, sC3, P_gn2g, P_gn2b, S_s2, S_b2,
                      B_b2bf, skip_partials=True)
        t.wait_ge(sS.h, sS.n)
        t.wait_ge(sWQ.h, w12a_ld)
        last = None
        for m in range(16):
            for kk in range(ND):
                last = t.matmul(psS[:, 32 + m:33 + m],
                                w1aTs[:, kk, m * 128:(m + 1) * 128],
                                B_b2bf[:, kk:kk + 1],
                                start=(kk == 0), stop=(kk == ND - 1))
        sT.inc(last)
        v.wait_ge(sT.h, sT.n)
        sV.inc(v.tensor_tensor(S_silu, psS[:, 32:48], P_b1ac, ALU.add))
        for kk in range(ND):
            v.tensor_scalar_mul(w1aTs[:, kk, :], w1aTs[:, kk, :], S_s2[:, kk:kk + 1])
        sV.inc(v.tensor_copy(bn6[:, 0:1], S_s2[:, 0:1]))
        w1a_scaled = sV.n
        tap("s2", S_s2); tap("b2", S_b2)

    xbrot = scrb[:, 0:2048]
    x0rot7 = scrb[:, 2048:6144].bitcast(F32).rearrange("p (a c) -> p a c", c=D)
    tmpf72 = [scrb[:, 6144:8192].bitcast(F32), scrb[:, 8192:10240].bitcast(F32)]
    ln2rot = scrb[:, 10240:11264]

    if PH >= 8:
        t.wait_ge(sV.h, w1a_scaled)
        sc.wait_ge(sV.h, w1a_scaled)
        xb_bounced = [0, 0, 0, 0]
        hps = [0, 0, 0, 0]
        ops = [0, 0, 0, 0]
        hs_read = 0
        p7 = dict(lnw=0, ln2_wr=0, x1_wr=0, ln2T_tb=[0, 0, 0, 0], first=True)

        def emit_p7_pair(pi):
            t0 = 2 * pi
            tb = t0 // 4
            if t0 % 4 == 0:
                sy.wait_ge(sXBt[tb].h, xb_bounced[tb])
            if p7["first"]:
                sy.wait_ge(sT.h, p4_mm_end)
                sc.wait_ge(sS.h, sS.n)
                p7["first"] = False
            if p7["lnw"]:
                sy.wait_ge(sV.h, p7["lnw"])
            for k in range(2):
                ti = t0 + k
                sFT[k].inc(sy.dma_start(xbrot[:, k * D:(k + 1) * D],
                                        xbscr[:, ti * 128:(ti + 1) * 128],
                                        transpose=True), 16)
            s0[0].inc(sy.dma_start(
                x0rot7, r128(x_nat[t0 * 128:(t0 + 2) * 128, :])), 16)
            for k in range(2):
                v.wait_ge(sFT[k].h, sFT[k].n)
                xbt = xbrot[:, k * D:(k + 1) * D]
                v.bn_stats(bn6[:, 12 * k:12 * k + 6], xbt[:, 0:512])
                ve(v.bn_stats(bn6[:, 12 * k + 6:12 * k + 12], xbt[:, 512:1024]))
                ve(v.bn_aggr(bn6[:, 48 + 2 * k:50 + 2 * k],
                             bn6[:, 12 * k:12 * k + 12]))
            sc.wait_ge(sV.h, sV.n)
            mv7 = bn6[:, 48:52].rearrange("p (a c) -> p a c", c=2)
            se(sc.activation(bn6[:, 56:58], mv7[:, :, 1], AF.Ln, bias=EPS))
            sS.inc(sc.activation(bn6[:, 56:58], bn6[:, 56:58], AF.Exp,
                                 scale=-0.5))
            v.wait_ge(sS.h, sS.n)
            if p7["x1_wr"]:
                v.wait_ge(sX1.h, p7["x1_wr"])
            for k in range(2):
                ti = t0 + k
                tmpk = tmpf72[k]
                ve(v.tensor_scalar(tmpk, xbrot[:, k * D:(k + 1) * D],
                                   bn6[:, 48 + 2 * k:49 + 2 * k],
                                   bn6[:, 56 + k:57 + k],
                                   ALU.subtract, ALU.mult))
                v.wait_ge(s0[0].h, s0[0].n)
                ve(v.tensor_tensor(tmpk, tmpk, x0rot7[:, k, :], ALU.add))
                sy.wait_ge(sV.h, sV.n)
                sX1.inc(sy.dma_start(x1scr[ti * 128:(ti + 1) * 128, :], tmpk),
                        16)
            p7["x1_wr"] = sX1.n
            p7["lnw"] = sV.n
            for k in range(2):
                v.bn_stats(bn6[:, 12 * k:12 * k + 6], tmpf72[k][:, 0:512])
                ve(v.bn_stats(bn6[:, 12 * k + 6:12 * k + 12],
                              tmpf72[k][:, 512:1024]))
                ve(v.bn_aggr(bn6[:, 52 + 2 * k:54 + 2 * k],
                             bn6[:, 12 * k:12 * k + 12]))
            sc.wait_ge(sV.h, sV.n)
            mv8 = bn6[:, 52:56].rearrange("p (a c) -> p a c", c=2)
            se(sc.activation(bn6[:, 58:60], mv8[:, :, 1], AF.Ln, bias=EPS))
            sS.inc(sc.activation(bn6[:, 58:60], bn6[:, 58:60], AF.Exp,
                                 scale=-0.5))
            v.wait_ge(sS.h, sS.n)
            for k in range(2):
                ti = t0 + k
                if p7["ln2_wr"]:
                    v.wait_ge(sLW[0].h, p7["ln2_wr"])
                sV.inc(v.tensor_scalar(ln2rot, tmpf72[k],
                                       bn6[:, 52 + 2 * k:53 + 2 * k],
                                       bn6[:, 58 + k:59 + k],
                                       ALU.subtract, ALU.mult))
                sy.wait_ge(sV.h, sV.n)
                sLW[0].inc(sy.dma_start(ln2scr[ti * 128:(ti + 1) * 128, :],
                                        ln2rot), 16)
                p7["ln2_wr"] = sLW[0].n
        def emit_p7_ln2T(tb):
            sy.wait_ge(sLW[0].h, sLW[0].n)
            for j in range(ND):
                sXBt[tb].inc(sy.dma_start(
                    ln2T[:, j, tb * 512:(tb + 1) * 512],
                    ln2scr[tb * 512:(tb + 1) * 512, j * 128:(j + 1) * 128],
                    transpose=True), 16)
            p7["ln2T_tb"][tb] = sXBt[tb].n

        xa3 = bigA[:, 12288:16384].rearrange("p (a c) -> p a c", c=512)
        for tb in range(4):
            if tb == 3:
                sV.inc(v.tensor_copy(xa3, x_aT[:, :, 1536:2048]))
                xa3_cp = sV.n
            for mh in range(16):
                reg = psA[mh % 4]
                if hps[mh % 4]:
                    t.wait_ge(sS.h, hps[mh % 4])
                last = None
                for kk in range(ND):
                    last = t.matmul(reg[:, :], w1aTs[:, kk, mh * 128:(mh + 1) * 128],
                                    x_aT[:, kk, tb * 512:(tb + 1) * 512],
                                    start=(kk == 0), stop=(kk == ND - 1))
                sT.inc(last)
                sc.wait_ge(sT.h, sT.n)
                if tb and mh == 0:
                    sc.wait_ge(sT.h, hs_read)
                sS.inc(sc.activation(hA[:, mh, :], reg[:, :], AF.Silu,
                                     bias=S_silu[:, mh:mh + 1]))
                hps[mh % 4] = sS.n
                if mh == 15:
                    h_ready = sS.n
                if PH >= 9 and tb >= 1 and mh == 9:
                    emit_p7_pair(2 * (tb - 1))
            if tb == 3:
                p6_hmm_end = sT.n
            t.wait_ge(sS.h, h_ready)
            for md in range(ND):
                reg = (psB[0], psB[1], psR, psS)[md % 4]
                if ops[md % 4]:
                    t.wait_ge(sS.h, ops[md % 4])
                for kk in range(16):
                    t.matmul(reg[:, :], w2aTs[:, kk, md * 128:(md + 1) * 128],
                             hA[:, kk, :], start=(kk == 0), stop=False)
                xsrc = (xa3[:, md, :] if tb == 3
                        else x_aT[:, md, tb * 512:(tb + 1) * 512])
                last = t.matmul(reg[:, :], ident_s.ap(), xsrc,
                                start=False, stop=True)
                sT.inc(last)
                sc.wait_ge(sT.h, sT.n)
                sS.inc(sc.activation(x_bT[:, md, tb * 512:(tb + 1) * 512],
                                     reg[:, :], AF.Identity,
                                     bias=P_b2ac[:, md:md + 1]))
                ops[md % 4] = sS.n
                if PH >= 9 and tb >= 1 and md == 2:
                    emit_p7_pair(2 * (tb - 1) + 1)
            hs_read = sT.n
            sc.wait_ge(sS.h, sS.n)
            sXBt[tb].inc(sc.dma_start(
                r128(xbscr[:, tb * 512:(tb + 1) * 512]),
                x_bT[:, :, tb * 512:(tb + 1) * 512]), 16)
            xb_bounced[tb] = sXBt[tb].n
            if PH >= 9 and tb >= 1:
                emit_p7_ln2T(tb - 1)
        p6_v_end = sV.n
        p6_mm_end = sT.n
        tap("xbT", bigD[:, :])
        if PH >= 9:
            ln2T_tb = p7["ln2T_tb"]
            if PH < 10:
                emit_p7_pair(6)
                emit_p7_pair(7)
                emit_p7_ln2T(3)
            tap("ln2T", bigD[:, :])

    ffrot4 = bigC[:, 0:4096].rearrange("p (a c) -> p a c", c=D)
    tmpf9 = bigC[:, 4096:6144].bitcast(F32)
    outrot4 = bigC[:, 8192:16384].bitcast(F32).rearrange("p (a c) -> p a c", c=D)
    w1f_bufs = [bigC[:, 0:8192].rearrange("p (a c) -> p a c", c=D),
                scrb[:, 0:8192].rearrange("p (a c) -> p a c", c=D)]
    w2f_bufs = [bigC[:, 8192:16384].rearrange("p (a c) -> p a c", c=D),
                bigA[:, 0:8192].rearrange("p (a c) -> p a c", c=D)]

    if PH >= 10:
        g.wait_ge(sV.h, xa3_cp)
        g.wait_ge(sT.h, p6_hmm_end)
        joinc(t)
        ffo_bounced = [0, 0, 0, 0]
        hq_ps = [0, 0, 0, 0]
        oq_s = [0, 0, 0, 0]
        oq_v = [0, 0, 0, 0]
        p9 = dict(ffread=0, o_dma=(0, 0), ld=([0, 0, 0, 0], 0), first=True)
        qd_mm_end = [0, 0, 0, 0]
        wq1_ld = [[(0, 0)] * ND for _ in range(4)]
        wq2_ld = [0, 0, 0, 0]
        g.wait_ge(sC4.h, 3)

        def load_quarter(qd):
            pb8 = qd % 2
            if qd >= 2:
                g.wait_ge(sT.h, qd_mm_end[qd - 2])
            for mh in range(ND):
                s_ = (sWQ, sWQb)[mh % 2]
                g.wait_ge(s_.h, s_.n)
                s_.inc(g.dma_start(
                    w1f_bufs[pb8][:, :, mh * 128:(mh + 1) * 128],
                    r128(sg1g_d[:, qd * 1024 + mh * 128:
                                qd * 1024 + (mh + 1) * 128])), 16)
                wq1_ld[qd][mh] = (mh % 2, s_.n)
            g.wait_ge(sWQ.h, sWQ.n)
            sWQ.inc(g.dma_start(
                w2f_bufs[pb8], r128(sg2g_d[qd * 1024:(qd + 1) * 1024, :])), 16)
            wq2_ld[qd] = sWQ.n

        def emit_p9_ldA(tb):
            sy.wait_ge(sXBt[tb].h, ffo_bounced[tb])
            if p9["first"]:
                sy.wait_ge(sV.h, p7["v_end"])
                sy.wait_ge(sS.h, p7["s_end"])
                sy.wait_ge(sX1.h, sX1.n)
                sy.wait_ge(sT.h, qd_mm_end[2])
                p9["first"] = False
            if p9["ffread"]:
                sy.wait_ge(sV.h, p9["ffread"])
            trld = [0, 0, 0, 0]
            tsem = (sFT[0], sFT[1], sKS, sVS)
            for i4 in range(4):
                ti = 4 * tb + i4
                tsem[i4].inc(sy.dma_start(ffrot4[:, i4, :],
                                          ffoscr[:, ti * 128:(ti + 1) * 128],
                                          transpose=True), 16)
                trld[i4] = tsem[i4].n
            if p9["o_dma"][0]:
                sy.wait_ge(sOT[1].h, p9["o_dma"][0])
                sy.wait_ge(sOT2.h, p9["o_dma"][1])
            sOT[0].inc(sy.dma_start(
                outrot4, r128(x1scr[tb * 512:(tb + 1) * 512, :])), 16)
            p9["ld"] = (trld, sOT[0].n)

        def emit_p9_prcB(tb):
            tsem = (sFT[0], sFT[1], sKS, sVS)
            for i4 in range(4):
                v.wait_ge(tsem[i4].h, p9["ld"][0][i4])
                v.bn_stats(bn6[:, 12 * i4:12 * i4 + 6], ffrot4[:, i4, 0:512])
                ve(v.bn_stats(bn6[:, 12 * i4 + 6:12 * i4 + 12],
                              ffrot4[:, i4, 512:1024]))
                ve(v.bn_aggr(bn6[:, 48 + 2 * i4:50 + 2 * i4],
                             bn6[:, 12 * i4:12 * i4 + 12]))
            sc.wait_ge(sV.h, sV.n)
            mv9 = bn6[:, 48:56].rearrange("p (a c) -> p a c", c=2)
            se(sc.activation(bn6[:, 56:60], mv9[:, :, 1], AF.Ln, bias=EPS))
            sS.inc(sc.activation(bn6[:, 56:60], bn6[:, 56:60], AF.Exp,
                                 scale=-0.5))
            v.wait_ge(sS.h, sS.n)
            v.wait_ge(sOT[0].h, p9["ld"][1])
            for i4 in range(4):
                ti = 4 * tb + i4
                ve(v.tensor_scalar(tmpf9, ffrot4[:, i4, :],
                                   bn6[:, 48 + 2 * i4:49 + 2 * i4],
                                   bn6[:, 56 + i4:57 + i4],
                                   ALU.subtract, ALU.mult))
                ve(v.tensor_tensor(outrot4[:, i4, :], tmpf9,
                                   outrot4[:, i4, :], ALU.add))
                sy.wait_ge(sV.h, sV.n)
                (sOT[1], sOT2)[i4 % 2].inc(
                    sy.dma_start(out_d[ti * 128:(ti + 1) * 128, :],
                                 outrot4[:, i4, :]), 16)
            p9["ffread"] = sV.n
            p9["o_dma"] = (sOT[1].n, sOT2.n)

        load_quarter(0)
        for qd in range(4):
            w1fq = w1f_bufs[qd % 2]
            w2fq = w2f_bufs[qd % 2]
            if 1 <= qd < 3:
                load_quarter(qd + 1)
            if qd == 0:
                sc.wait_ge(sV.h, alpha_done)
                v.wait_ge(sV.h, alpha_done)
            for tb in range(4):
                if qd == 0:
                    t.wait_ge(sXBt[tb].h, ln2T_tb[tb])
                for mh in range(ND):
                    reg = psA[mh % 4]
                    if tb == 0:
                        wsem, wcnt = wq1_ld[qd][mh]
                        t.wait_ge((sWQ, sWQb)[wsem].h, wcnt)
                    if hq_ps[mh % 4]:
                        t.wait_ge(sS.h, hq_ps[mh % 4])
                    last = None
                    for kk in range(ND):
                        last = t.matmul(reg[:, :], w1fq[:, kk, mh * 128:(mh + 1) * 128],
                                        ln2T[:, kk, tb * 512:(tb + 1) * 512],
                                        start=(kk == 0), stop=(kk == ND - 1))
                    sT.inc(last)
                    sc.wait_ge(sT.h, sT.n)
                    sS.inc(sc.activation(hq[:, mh, :], reg[:, :], AF.Gelu,
                                         scale=S_a1[:, qd * 8 + mh:qd * 8 + mh + 1],
                                         bias=P_b1fc[:, qd * 8 + mh:qd * 8 + mh + 1]))
                    hq_ps[mh % 4] = sS.n
                    if mh == ND - 1:
                        hq_ready = sS.n
                    if qd == 0 and tb == 0 and mh == 1:
                        emit_p7_pair(6)
                    if PH >= 11 and qd == 3 and tb >= 1 and mh == 1:
                        emit_p9_ldA(tb - 1)
                t.wait_ge(sS.h, hq_ready)
                if tb == 0:
                    t.wait_ge(sWQ.h, wq2_ld[qd])
                for md in range(ND):
                    reg = (psB[0], psB[1], psR, psS)[md % 4]
                    if oq_s[md % 4]:
                        t.wait_ge(sS.h, oq_s[md % 4])
                    if oq_v[md % 4]:
                        t.wait_ge(sV.h, oq_v[md % 4])
                    last = None
                    for kk in range(ND):
                        last = t.matmul(reg[:, :], w2fq[:, kk, md * 128:(md + 1) * 128],
                                        hq[:, kk, :], start=(kk == 0),
                                        stop=(kk == ND - 1 and qd != 3))
                    dst = ffoT[:, md, tb * 512:(tb + 1) * 512]
                    if qd == 3:
                        last = t.matmul(reg[:, :], ident_s.ap(), dst,
                                        start=False, stop=True)
                    sT.inc(last)
                    if qd == 0:
                        sc.wait_ge(sT.h, sT.n)
                        sS.inc(sc.copy(dst, reg[:, :]))
                        oq_s[md % 4] = sS.n
                    elif qd < 3:
                        v.wait_ge(sT.h, sT.n)
                        sV.inc(v.tensor_tensor(dst, reg[:, :], dst, ALU.add))
                        oq_v[md % 4] = sV.n
                    else:
                        sc.wait_ge(sT.h, sT.n)
                        sS.inc(sc.activation(dst, reg[:, :], AF.Identity,
                                             scale=S_a2[:, md:md + 1],
                                             bias=P_b2fc[:, md:md + 1]))
                        oq_s[md % 4] = sS.n
                    if PH >= 11 and qd == 3 and tb >= 1 and md == 1:
                        emit_p9_prcB(tb - 1)
                    if qd == 0 and tb == 1 and md == 1:
                        emit_p7_pair(7)
                    if qd == 0 and tb == 1 and md == 4:
                        emit_p7_ln2T(3)
                qd_mm_end[qd] = sT.n
                if qd == 0 and tb == 1:
                    p7["v_end"] = sV.n
                    p7["s_end"] = sS.n
                    g.wait_ge(sV.h, p7["v_end"])
                    g.wait_ge(sS.h, p7["s_end"])
                    for s_ in (sFT[0], sFT[1], s0[0], s0[1], sLW[0], sX1):
                        if s_.n:
                            g.wait_ge(s_.h, s_.n)
                    load_quarter(1)
                if qd == 3:
                    sc.wait_ge(sS.h, sS.n)
                    sXBt[tb].inc(sc.dma_start(
                        r128(ffoscr[:, tb * 512:(tb + 1) * 512]),
                        ffoT[:, :, tb * 512:(tb + 1) * 512]), 16)
                    ffo_bounced[tb] = sXBt[tb].n
            v.wait_ge(sV.h, sV.n)
        tap("ffoT", bigE[:, :])
        if PH >= 11:
            emit_p9_ldA(3)
            emit_p9_prcB(3)

    if PH < 11:
        joinc(v, sy)
        zf = scrb[:, 0:2048].bitcast(F32)
        sV.inc(v.memset(zf, 0.0))
        sy.wait_ge(sV.h, sV.n)
        for ti in range(NT):
            sOT[ti % 2].inc(sy.dma_start(out_d[ti * 128:(ti + 1) * 128, :],
                                         zf), 16)

    for s in ALL_DMA_SEMS:
        if s.n:
            sy.wait_ge(s.h, s.n)
    return nc


def _stage_inputs(inputs):
    x = np.asarray(inputs["L0_attn_x"], np.float32).reshape(16384, 1024)
    tw = lambda w: np.ascontiguousarray(np.asarray(w, np.float32).T)
    shared = {
        "qwT": tw(inputs["L0_attn_qw"]), "kwT": tw(inputs["L0_attn_kw"]),
        "vwT": tw(inputs["L0_attn_vw"]), "owT": tw(inputs["L0_attn_ow"]),
        "w1aT": tw(inputs["L0_attn_w1"]), "w2aT": tw(inputs["L0_attn_w2"]),
        "gn1g": _col128(inputs["L0_attn_norm_g"]), "gn1b": _col128(inputs["L0_attn_norm_b"]),
        "gn2g": _col128(inputs["L0_attn_ffn_norm_g"]), "gn2b": _col128(inputs["L0_attn_ffn_norm_b"]),
        "obc": _col128(inputs["L0_attn_ob"]), "b1ac": _col128(inputs["L0_attn_b1"]),
        "b2ac": _col128(inputs["L0_attn_b2"]), "b1fc": _col128(inputs["L0_ffn_b1"]),
        "b2fc": _col128(inputs["L0_ffn_b2"]),
    }
    w1T = tw(inputs["L0_ffn_w1"]); w2T = tw(inputs["L0_ffn_w2"])
    w1n = np.ascontiguousarray(np.asarray(inputs["L0_ffn_w1"], np.float32))
    w2n = np.ascontiguousarray(np.asarray(inputs["L0_ffn_w2"], np.float32))
    maps = []
    for r in range(NCORES):
        m = dict(shared)
        m["x_nat"] = np.ascontiguousarray(x[r * T:(r + 1) * T])
        m["w1slab"] = np.ascontiguousarray(w1T[r * 128:(r + 1) * 128])
        m["w2slab"] = np.ascontiguousarray(w2T[r * 512:(r + 1) * 512])
        m["w1nslab"] = np.ascontiguousarray(w1n[r * 512:(r + 1) * 512])
        m["w2nslab"] = np.ascontiguousarray(w2n[r * 128:(r + 1) * 128])
        m.update(_host_consts(r))
        maps.append(m)
    return maps


def kernel(**inputs):
    maps = _stage_inputs(inputs)
    nc = build_nc()
    res = run_bass_kernel_spmd(nc, maps, core_ids=list(range(NCORES)))
    outs = [res.results[r]["out"] for r in range(NCORES)]
    full = np.concatenate(outs, axis=0).reshape(1, 16384, 1024).astype(np.float32)
    if _TAPS:
        kernel._tap_results = [{k: res.results[r].get("tap_" + k) for k in _TAPS}
                               for r in range(NCORES)]
    return full
